# version 22
# baseline (speedup 1.0000x reference)
import math
from contextlib import ExitStack

import numpy as np

import concourse.bass as bass
import concourse.mybir as mybir
from concourse.bass import AP
from concourse.bass_utils import run_bass_kernel_spmd

F32 = mybir.dt.float32
F32R = mybir.dt.float32r
ALU = mybir.AluOpType
ACTF = mybir.ActivationFunctionType
AX = mybir.AxisListType

T = 4096
D = 1024
KD = 128
V = 2048
I = 3072
NB = 8
NT = 32
DC = 8
IFC = 24
VFC = 16

M23 = 1.5 * 2.0 ** 23
M52 = 1.5 * 2.0 ** 52
_IT = 1.0 / math.sqrt(128.0)
INV_TAU_HI = float(np.float32(_IT))
INV_TAU_LO = float(np.float32(_IT - INV_TAU_HI))
_I7 = 1.0 / 7.0
INV7_HI = float(np.float32(_I7))
INV7_LO = float(np.float32(_I7 - INV7_HI))
_IQ = 1.0 / 32767.0
INV_QMAX_HI = float(np.float32(_IQ))
INV_QMAX_LO = float(np.float32(_IQ - INV_QMAX_HI))

ENGS = ("sync", "vector", "scalar", "tensor", "gpsimd")
DMA_ENGS = ("sync", "scalar", "gpsimd")


def _host_constants():
    try:
        import jax
        import jax.numpy as jnp
        cpu = jax.devices("cpu")[0]
        with jax.default_device(cpu):
            pos = jnp.arange(T, dtype=jnp.float32)
            inv_lams = 1.0 / 10000.0 ** (jnp.arange(0, D, 2, dtype=jnp.float32) / D)
            pre = pos[:, None] * inv_lams[None, :]
            cat = jnp.concatenate([jnp.sin(pre), jnp.cos(pre)], axis=-1)
            emb = jnp.flip(cat, axis=0)
            embT = np.ascontiguousarray(np.asarray(emb).T.astype(np.float32))
    except Exception:
        pos = np.arange(T, dtype=np.float32)
        expo = (np.arange(0, D, 2).astype(np.float32) / np.float32(D)).astype(np.float32)
        inv_lams = (np.float32(1.0) / np.power(np.float32(10000.0), expo)).astype(np.float32)
        pre = (pos[:, None] * inv_lams[None, :]).astype(np.float32)
        cat = np.concatenate([np.sin(pre).astype(np.float32), np.cos(pre).astype(np.float32)], axis=1)
        embT = np.ascontiguousarray(cat[::-1].T.astype(np.float32))
    u = np.arange(1024)[None, :]
    pp = np.arange(128)[:, None]
    hmask = np.where(u <= pp + 512, np.float32(0.0), np.float32(-1e30)).astype(np.float32)
    ident = np.eye(128, dtype=np.float32)
    return embT, hmask, ident


class Prog:
    def __init__(self, nc):
        self.nc = nc
        self.steps = {e: [] for e in ENGS}
        self.sems = {}
        self.cnt = {}
        for e in ENGS:
            self.sems[e] = nc.alloc_semaphore(f"S_{e}")
            self.cnt[e] = 0

    def op(self, eng, fn, waits=(), inc=True):
        tok = None
        if inc:
            self.cnt[eng] += 1
            tok = (eng, self.cnt[eng])
        ws = tuple(w for w in waits if w is not None)
        self.steps[eng].append((ws, fn, eng if inc else None))
        return tok

    def dma(self, eng, out_ap, in_ap, waits=(), ch="misc", **kw):
        key = "D:" + ch
        if key not in self.sems:
            self.sems[key] = self.nc.alloc_semaphore("SD_" + ch)
            self.cnt[key] = 0
        self.cnt[key] += 16
        tok = (key, self.cnt[key])
        ws = tuple(w for w in waits if w is not None)

        def fn(E, o=out_ap, i=in_ap, kw=kw):
            return E.dma_start(out=o, in_=i, **kw)

        self.steps[eng].append((ws, fn, key))
        return tok

    def fence(self, eng, waits=()):
        if self.cnt[eng] == 0:
            return None
        return (eng, self.cnt[eng])

    def waits_only(self, eng, waits):
        ws = tuple(w for w in waits if w is not None)
        if ws:
            self.steps[eng].append((ws, None, None))

    def barrier(self):
        toks = [self.fence(e) for e in ENGS]
        dtoks = [(k, v) for k, v in self.cnt.items() if k.startswith("D:") and v > 0]
        for e in ENGS:
            self.waits_only(e, [t for t in toks if t is not None and t[0] != e] + dtoks)

    def build_block(self, block):
        for e in ENGS:
            steps = self.steps[e]
            sems = self.sems

            def section(eng, steps=steps, sems=sems):
                seen = {}
                for ws, fn, inckey in steps:
                    for (k, v) in ws:
                        if seen.get(k, 0) >= v:
                            continue
                        eng.wait_ge(sems[k], v)
                        seen[k] = v
                    if fn is None:
                        continue
                    inst = fn(eng)
                    if inckey is not None:
                        inst.then_inc(sems[inckey], 16 if inckey.startswith("D:") else 1)

            getattr(block, e)(section)


class Banks:
    def __init__(self):
        self.cons = {i: [] for i in range(8)}

    def wr(self, i):
        w = self.cons[i]
        self.cons[i] = []
        return list(w)

    def rd(self, i, tok):
        if tok is not None:
            self.cons[i].append(tok)


def build(n_cores=8, dbg=(), phases=9):
    nc = bass.Bass("TRN2", target_bir_lowering=False, detect_race_conditions=False)
    p = Prog(nc)
    bk = Banks()

    x_in = nc.dram_tensor("x", [T, D], F32, kind="ExternalInput").ap()
    gln_in = nc.dram_tensor("g_ln", [D], F32, kind="ExternalInput").ap()
    wq_in = nc.dram_tensor("wq", [D, KD], F32R, kind="ExternalInput").ap()
    wk_in = nc.dram_tensor("wk", [D, KD], F32R, kind="ExternalInput").ap()
    wv_in = nc.dram_tensor("wv", [D, V], F32R, kind="ExternalInput").ap()
    wg_in = nc.dram_tensor("wg", [D, V], F32R, kind="ExternalInput").ap()
    wr_in = nc.dram_tensor("wr", [D, KD], F32R, kind="ExternalInput").ap()
    xlu_in = nc.dram_tensor("xlu", [KD], F32, kind="ExternalInput").ap()
    xlv_in = nc.dram_tensor("xlv", [KD], F32, kind="ExternalInput").ap()
    wo_in = nc.dram_tensor("wo", [V, D], F32R, kind="ExternalInput").ap()
    wup_in = nc.dram_tensor("wup", [I, D], F32, kind="ExternalInput").ap()
    wgate_in = nc.dram_tensor("wgate", [I, D], F32, kind="ExternalInput").ap()
    wdown_in = nc.dram_tensor("wdown", [D, I], F32, kind="ExternalInput").ap()
    embt_in = nc.dram_tensor("embT", [D, T], F32R, kind="ExternalInput").ap()
    hmask_in = nc.dram_tensor("hmask", [128, 1024], F32, kind="ExternalInput").ap()
    ident_in = nc.dram_tensor("ident", [128, 128], F32, kind="ExternalInput").ap()
    y_out = nc.dram_tensor("y", [T, D], F32, kind="ExternalOutput").ap()
    dbg_outs = {}
    for name, shape in dbg:
        dbg_outs[name] = nc.dram_tensor("dbg_" + name, list(shape), F32, kind="ExternalOutput").ap()

    v_hbm = nc.dram_tensor("v_hbm", [NT, 128, V], F32R, kind="Internal")
    g_hbm = nc.dram_tensor("g_hbm", [NT, 128, V], F32R, kind="Internal")
    bd_hbm = nc.dram_tensor("bd_hbm", [128 * T + 1024], F32, kind="Internal")
    ln1_hbm = nc.dram_tensor("ln1_hbm", [NT, 128, D], F32, kind="Internal")
    upraw_hbm = nc.dram_tensor("upraw_hbm", [IFC, 128, T], F32, kind="Internal")
    gateraw_hbm = nc.dram_tensor("gateraw_hbm", [IFC, 128, T], F32, kind="Internal")
    opre_hbm = nc.dram_tensor("opre_hbm", [IFC, 128, T], F32, kind="Internal")
    tq_hbm = nc.dram_tensor("tq_hbm", [IFC, 128, T], F32, kind="Internal")
    wupT_hbm = nc.dram_tensor("wupT_hbm", [DC, 128, I], F32R, kind="Internal")
    wgateT_hbm = nc.dram_tensor("wgateT_hbm", [DC, 128, I], F32R, kind="Internal")
    wdownT_hbm = nc.dram_tensor("wdownT_hbm", [IFC, 128, D], F32R, kind="Internal")
    cc_bufs = {}
    for tag in ("a", "b1", "b2", "c", "d"):
        cc_bufs[tag] = (
            nc.dram_tensor(f"cc_{tag}_in", [128], F32, kind="Internal"),
            nc.dram_tensor(f"cc_{tag}_out", [128], F32, kind="Internal"),
            nc.dram_tensor(f"cc_{tag}_sc", [2], F32, kind="Internal"),
        )


    def _finish():
        fw = [(k, v) for k, v in p.cnt.items() if k.startswith("D:") and v > 0]
        for e in ("vector", "scalar", "tensor"):
            fw.append(p.fence(e))
        p.waits_only("gpsimd", fw)
        with nc.Block() as block:
            p.build_block(block)
        return nc
    g_stack = ExitStack()

    def sb(name, shape, dt=F32):
        return g_stack.enter_context(nc.sbuf_tensor(name, shape, dt))

    identt = sb("identt", [128, 128])
    identr = sb("identr", [128, 128], F32R)
    gln = sb("gln", [128, DC])
    xlu = sb("xlu_t", [128, 1])
    xlv = sb("xlv_t", [128, 1])
    det = sb("det", [128, 1])
    ones1 = sb("ones1", [128, 1])
    biasA = sb("biasA", [128, 1])
    m1t = sb("m1t", [128, 1])
    m2t = sb("m2t", [128, 1])
    invtau = sb("invtau", [128, 1])
    invqmax = sb("invqmax", [1, 1])
    ones_row = sb("ones_row", [1, 128])
    scl = {}
    for tag in ("s1", "s2", "s3", "s4", "s5"):
        scl[tag] = sb(tag + "_b", [128, 1])
        scl["i" + tag] = sb("i" + tag + "_b", [128, 1])
    lnmax = sb("lnmax", [128, 1])
    upmax = sb("upmax", [128, 1])
    gatemax = sb("gatemax", [128, 1])
    tmaxc = sb("tmaxc", [128, 1])
    omaxc = sb("omaxc", [128, 1])
    sc_row = sb("sc_row", [1, 128])
    sc_one = sb("sc_one", [1, 1])
    sc_tmp = sb("sc_tmp", [1, 1])
    tredA = sb("tredA", [128, 1])
    tredE = sb("tredE", [128, 1])
    tredE2 = sb("tredE2", [128, 1])
    eps6 = sb("eps6", [128, 1])
    eps5 = sb("eps5", [128, 1])
    c64k = sb("c64k", [128, 1])
    c1k = sb("c1k", [128, 1])

    attn_stack = ExitStack()

    def sba(name, shape, dt=F32):
        return attn_stack.enter_context(nc.sbuf_tensor(name, shape, dt))

    hmaskt = sba("hmaskt", [128, 1024])
    qu = sba("qu", [128, T], F32R)
    kfm = sba("kfm", [128, T], F32R)
    xlr = sba("xlr", [128, T], F32R)

    s1_stack = ExitStack()

    def sb1(name, shape, dt=F32):
        return s1_stack.enter_context(nc.sbuf_tensor(name, shape, dt))

    p0_stack = ExitStack()

    def sb0(name, shape, dt=F32):
        return p0_stack.enter_context(nc.sbuf_tensor(name, shape, dt))

    wq_t = [sb1(f"wq{dc}", [128, KD], F32R) for dc in range(DC)]
    wk_t = [sb1(f"wk{dc}", [128, KD], F32R) for dc in range(DC)]
    wr_t = [sb1(f"wr{dc}", [128, KD], F32R) for dc in range(DC)]
    rr_col = sb1("rr_col", [128, NT])

    ps = [g_stack.enter_context(nc.psum_tensor(f"p{i}", [128, 512], F32)) for i in range(8)]

    import kernel as _K1
    if "cst1" not in getattr(_K1, "SKIP", set()):
        p.dma("sync", identt[:], ident_in, ch="cst")
    p.dma("sync", hmaskt[:], hmask_in, ch="cst")
    for dc in range(DC):
        p.dma("sync", gln[:, dc:dc + 1],
              gln_in[dc * 128:(dc + 1) * 128].rearrange("(p o) -> p o", o=1), ch="cst")
    p.dma("sync", xlu[:], xlu_in.rearrange("(p o) -> p o", o=1), ch="cst")
    p.dma("sync", xlv[:], xlv_in.rearrange("(p o) -> p o", o=1), ch="cst")

    import kernel as _K0
    _SKIP = getattr(_K0, "SKIP", set())
    t0 = p.op("vector", lambda E: E.memset(det[:], 2.0 ** -30))
    t1 = p.op("vector", lambda E: E.memset(ones1[:], 1.0))
    t2 = p.op("vector", lambda E: E.tensor_scalar(det[:], det[:], 1.0, None, ALU.add), waits=[t0])
    t3 = p.op("vector", lambda E: E.tensor_tensor(det[:], det[:], ones1[:], ALU.is_equal), waits=[t1, t2])
    t4 = p.op("vector", lambda E: E.tensor_scalar(m1t[:], det[:], M23, None, ALU.mult), waits=[t3])
    t5 = p.op("vector", lambda E: E.tensor_scalar(m2t[:], det[:], -M52, M52, ALU.mult, ALU.add), waits=[t3])
    t_biasA = p.op("vector", lambda E: E.tensor_add(biasA[:], m1t[:], m2t[:]), waits=[t4, t5])
    t6 = p.op("vector", lambda E: E.memset(invtau[:], INV_TAU_HI))
    t_invtau = p.op("vector", lambda E: E.tensor_scalar(invtau[:], invtau[:], INV_TAU_LO, None, ALU.add),
                    waits=[t6])
    t7 = p.op("vector", lambda E: E.memset(invqmax[:], INV_QMAX_HI))
    t_invq = p.op("vector", lambda E: E.tensor_scalar(invqmax[:], invqmax[:], INV_QMAX_LO, None, ALU.add),
                  waits=[t7])
    t_or = p.op("vector", lambda E: E.memset(ones_row[:], 1.0))
    t_c6 = p.op("vector", lambda E: E.memset(eps6[:], 1e-6))
    t_c5 = p.op("vector", lambda E: E.memset(eps5[:], 1e-5))
    t_c64k = p.op("vector", lambda E: E.memset(c64k[:], 65536.0))
    t_c1k = p.op("vector", lambda E: E.memset(c1k[:], 1.0 / 1024.0))
    t_mx0 = {}
    for tag, tile in (("ln", lnmax), ("up", upmax), ("gate", gatemax), ("t", tmaxc), ("o", omaxc)):
        t_mx0[tag] = p.op("vector", lambda E, tile=tile: E.memset(tile[:], 0.0))


    if phases <= -3:
        return _finish()
    xt_l = [sb0("p0x0", [128, D]), sb0("p0x1", [128, D])]
    sq_l = [sb0("p0sq0", [128, D]), sb0("p0sq1", [128, D])]
    ms_col = sb0("ms_col", [128, NT])
    t_sq = [None, None]
    t_ms = [None, None]
    t_ms_all = []
    for tt in range(NT if "rrms" not in _SKIP else 0):
        i = tt % 2
        t_x = p.dma("sync", xt_l[i][:], x_in[tt * 128:(tt + 1) * 128, :], waits=[t_ms[i]], ch=f"p0x{i}")
        t_sq[i] = p.op("scalar", lambda E, a=sq_l[i], b=xt_l[i]: E.activation(a[:], b[:], ACTF.Square),
                       waits=[t_x])
        t_ms[i] = p.op("vector", lambda E, o=ms_col[:, tt:tt + 1], a=sq_l[i]: E.reduce_sum(o, a[:], AX.X),
                       waits=[t_sq[i]])
        t_ms_all.append(t_ms[i])
    sd_col = sb0("sd_col", [128, NT])
    if "rrms" not in _SKIP:
        t_sd = p.op("scalar", lambda E: E.activation(sd_col[:], ms_col[:], ACTF.Sqrt,
                                                     bias=eps6[:], scale=c1k[:]),
                    waits=t_ms_all + [t_c6, t_c1k])
        t_rc = p.op("vector", lambda E: E.reciprocal(rr_col[:], sd_col[:]), waits=[t_sd])
    else:
        t_rc = None

    if phases <= -2:
        return _finish()
    for dc in range(DC):
        p.dma("sync", wq_t[dc][:], wq_in[dc * 128:(dc + 1) * 128, :], ch="cst")
        p.dma("sync", wk_t[dc][:], wk_in[dc * 128:(dc + 1) * 128, :], ch="cst")
        t_cst = p.dma("sync", wr_t[dc][:], wr_in[dc * 128:(dc + 1) * 128, :], ch="cst")

    t_idr = p.op("vector", lambda E: E.tensor_copy(identr[:], identt[:]), waits=[t_cst])
    emb_l = [sb0("emb0", [128, 512], F32R), sb0("emb1", [128, 512], F32R)]
    emb_rdr = [None, None]
    if phases <= -1:
        return _finish()
    import kernel as _K
    for pc in range(getattr(_K, "XLR_PCS", 8)):
        mm_last = None
        for dc in range(DC):
            i = (pc * DC + dc) % 2
            t_e = p.dma("sync", emb_l[i][:],
                        embt_in[dc * 128:(dc + 1) * 128, pc * 512:(pc + 1) * 512],
                        waits=[emb_rdr[i]], ch=f"emb{i}")
            mm_last = p.op("tensor", lambda E, dc=dc, i=i: E.matmul(
                ps[2][:, 0:512], wr_t[dc][:], emb_l[i][:],
                start=(dc == 0), stop=(dc == DC - 1)),
                waits=[t_e, t_cst] + (bk.wr(2) if dc == 0 else []))
            emb_rdr[i] = mm_last
        ev = p.op("vector", lambda E, pc=pc: E.tensor_copy(xlr[:, pc * 512:(pc + 1) * 512], ps[2][:, 0:512]),
                  waits=[mm_last])
        bk.rd(2, ev)

    if phases <= 0:
        return _finish()
    p.barrier()
    p0_stack.close()

    xband = [sb1(f"xb{i}", [128, D]) for i in range(4)]
    xtb = [sb1(f"xtb{dc}", [128, 512], F32R) for dc in range(DC)]
    wbig = [sb1(f"wbig{dc}", [128, V], F32R) for dc in range(DC)]
    vstage = [sb1(f"vst{i}", [128, V], F32R) for i in range(2)]
    gstage = vstage

    def emit_xt_build(b, f_tensor_prev):
        t_xb = []
        for ts in range(4):
            tt = b * 4 + ts
            td = p.dma("sync", xband[ts][:], x_in[tt * 128:(tt + 1) * 128, :],
                       waits=[f_tensor_prev], ch=f"xb{ts}")
            t_xb.append(p.op("vector", lambda E, ts=ts, tt=tt: E.tensor_scalar(
                xband[ts][:], xband[ts][:], rr_col[:, tt:tt + 1], None, ALU.mult),
                waits=[td, t_rc]))
        toks = {}
        for ts in range(4):
            for dc in range(DC):
                bank = 6 + ((ts * DC + dc) % 2)
                ttp = p.op("tensor", lambda E, ts=ts, dc=dc, bank=bank: E.transpose(
                    ps[bank][:, 0:128], xband[ts][:, dc * 128:(dc + 1) * 128], identt[:]),
                    waits=[t_xb[ts], t_cst] + bk.wr(bank))
                tev = p.op("scalar", lambda E, ts=ts, dc=dc, bank=bank: E.activation(
                    xtb[dc][:, ts * 128:(ts + 1) * 128], ps[bank][:, 0:128],
                    ACTF.Identity, bias=0.0, scale=gln[:, dc:dc + 1]),
                    waits=[ttp, t_cst])
                bk.rd(bank, tev)
                toks[(ts, dc)] = tev
        return toks

    t_wv = None
    for dc in range(DC):
        t_wv = p.dma("sync", wbig[dc][:], wv_in[dc * 128:(dc + 1) * 128, :], ch="wvload")
    f_tensor = None
    t_vst = [None, None]
    vst_evs = [[], []]
    for b in range(NB):
        t_xt = emit_xt_build(b, f_tensor)
        cols = slice(b * 512, (b + 1) * 512)
        mmq = mmk = None
        for dc in range(DC):
            w = [t_xt[(ts, dc)] for ts in range(4)]
            mmq = p.op("tensor", lambda E, dc=dc: E.matmul(
                ps[0][:, 0:512], wq_t[dc][:], xtb[dc][:],
                start=(dc == 0), stop=(dc == DC - 1)),
                waits=w + [t_cst] + (bk.wr(0) if dc == 0 else []))
            mmk = p.op("tensor", lambda E, dc=dc: E.matmul(
                ps[1][:, 0:512], wk_t[dc][:], xtb[dc][:],
                start=(dc == 0), stop=(dc == DC - 1)),
                waits=[t_cst] + (bk.wr(1) if dc == 0 else []))
        tq1 = p.op("vector", lambda E, cols=cols: E.tensor_scalar(
            qu[:, cols], ps[0][:, 0:512], invtau[:], xlu[:], ALU.mult, ALU.add),
            waits=[mmq, t_invtau, t_cst])
        bk.rd(0, tq1)
        tk1 = p.op("scalar", lambda E, cols=cols: E.copy(kfm[:, cols], ps[1][:, 0:512]), waits=[mmk])
        bk.rd(1, tk1)
        for ts in range(4):
            i = ts % 2
            evs = []
            for vc in range(4):
                bank = 2 + vc
                mmv = None
                for dc in range(DC):
                    mmv = p.op("tensor", lambda E, ts=ts, vc=vc, dc=dc, bank=bank: E.matmul(
                        ps[bank][:, 0:512], xtb[dc][:, ts * 128:(ts + 1) * 128],
                        wbig[dc][:, vc * 512:(vc + 1) * 512],
                        start=(dc == 0), stop=(dc == DC - 1)),
                        waits=[t_xt[(ts, dc)], t_wv] + (bk.wr(bank) if dc == 0 else []))
                ev = p.op("scalar", lambda E, i=i, vc=vc, bank=bank: E.copy(
                    vstage[i][:, vc * 512:(vc + 1) * 512], ps[bank][:, 0:512]),
                    waits=[mmv, t_vst[i]])
                bk.rd(bank, ev)
                evs.append(ev)
            t_vst[i] = p.dma("scalar", v_hbm.ap()[b * 4 + ts], vstage[i][:], waits=evs, ch=f"vst{i}")
            vst_evs[i] = evs
        f_tensor = p.fence("tensor")

    t_wg = None
    for dc in range(DC):
        t_wg = p.dma("sync", wbig[dc][:], wg_in[dc * 128:(dc + 1) * 128, :],
                     waits=[f_tensor], ch="wvload")
    t_gst = [t_vst[0], t_vst[1]]
    for b in range(NB):
        t_xt = emit_xt_build(b, f_tensor)
        for ts in range(4):
            i = ts % 2
            evs = []
            for vc in range(4):
                bank = 2 + vc
                mmg = None
                for dc in range(DC):
                    mmg = p.op("tensor", lambda E, ts=ts, vc=vc, dc=dc, bank=bank: E.matmul(
                        ps[bank][:, 0:512], xtb[dc][:, ts * 128:(ts + 1) * 128],
                        wbig[dc][:, vc * 512:(vc + 1) * 512],
                        start=(dc == 0), stop=(dc == DC - 1)),
                        waits=[t_xt[(ts, dc)], t_wg] + (bk.wr(bank) if dc == 0 else []))
                ev = p.op("scalar", lambda E, i=i, vc=vc, bank=bank: E.copy(
                    gstage[i][:, vc * 512:(vc + 1) * 512], ps[bank][:, 0:512]),
                    waits=[mmg, t_gst[i]])
                bk.rd(bank, ev)
                evs.append(ev)
            t_gst[i] = p.dma("scalar", g_hbm.ap()[b * 4 + ts], gstage[i][:], waits=evs, ch=f"vst{i}")
        f_tensor = p.fence("tensor")

    if phases <= 1:
        if "qu" in dbg_outs:
            p.dma("gpsimd", dbg_outs["qu"], qu[:], waits=[p.fence("vector")], ch="dbg")
            p.dma("gpsimd", dbg_outs["kfm"], kfm[:], waits=[p.fence("scalar")], ch="dbg")
            p.dma("gpsimd", dbg_outs["xlr"], xlr[:], waits=[], ch="dbg")
            p.dma("gpsimd", dbg_outs["v0"], v_hbm.ap()[0], waits=[t_vst[0], t_vst[1]], ch="dbg")
            p.dma("gpsimd", dbg_outs["g0"], g_hbm.ap()[0], waits=[t_gst[0], t_gst[1]], ch="dbg")
        return _finish()

    p.barrier()
    s1_stack.close()

    s2_stack = ExitStack()

    def sb2(name, shape, dt=F32):
        return s2_stack.enter_context(nc.sbuf_tensor(name, shape, dt))

    band = [sb2(f"band{s}", [128, T], F32R) for s in range(4)]
    vbuf = [sb2(f"vb{i}", [128, 512], F32R) for i in range(3)]
    wvt = [sb2(f"wvg{s}", [128, V], F32R) for s in range(4)]
    gbuf = [sb2(f"gb{i}", [128, 1024], F32R) for i in range(2)]
    wobuf = [sb2(f"wob{i}", [128, 512], F32R) for i in range(2)]
    BDP = 4608
    bdraw = [sb2("bdraw0", [128, BDP])]
    ln1t = [sb2(f"ln1t{s}", [128, D]) for s in range(4)]
    qvs = [sb2(f"qvs{i}", [128, 128], F32R) for i in range(4)]
    dxl = sb2("dxl", [128, 1])
    mx = [sb2(f"mx{s}", [128, 1]) for s in range(4)]
    nmx = [sb2(f"nmx{s}", [128, 1]) for s in range(4)]
    rsum = [sb2(f"rsum{s}", [128, 1]) for s in range(4)]
    risum = [sb2(f"risum{s}", [128, 1]) for s in range(4)]
    smean = sb2("smean", [128, 1])
    smsq = sb2("smsq", [128, 1])
    svar = sb2("svar", [128, 1])
    srstd = sb2("srstd", [128, 1])

    t_bdz0 = p.op("vector", lambda E: E.memset(bdraw[0][:], 0.0))
    t_dxl = p.op("vector", lambda E: E.tensor_sub(dxl[:], xlv[:], xlu[:]), waits=[t_cst])
    t_diagp = [t_bdz0]
    vb_rdr = [None] * 4
    band_rdr = [None] * 4
    qvs_rdr = [None] * 4
    S_exp = [None, None]
    t_vb = [None] * 4
    t_gb = [None, None]
    t_wob = [None] * 4
    t_ln1 = None
    t_lnmax = t_mx0["ln"]
    t_sqt_rdr = None
    t_srstd_rdr = None
    t_gate_hist = []
    tsum = None
    t_tp_ev = {}
    t_diag = {}
    exp_last = [None] * 4
    mx_rdr = [None] * 4
    acst_rdr = [None] * 4
    rs_prev = [None] * 4
    ev_last = [None] * 4
    sq_rdr = None
    ln1t_rdr = [[] for _ in range(4)]
    wo_rdr = [None] * 4

    def emit_scores(b2):
        KC = b2 + 1
        W = 512 * KC

        def acst(j):
            return gbuf[j // 2][:, (j % 2) * 512:((j % 2) + 1) * 512]
        gh_tok = t_gate_hist[-1] if t_gate_hist else None
        for j in range(4):
            if acst_rdr[j] is None:
                acst_rdr[j] = gh_tok
            else:
                acst_rdr[j] = acst_rdr[j]
        tqv = [None] * 4
        for s in range(4):
            a = 512 * b2 + 128 * s
            tqv[s] = p.op("vector", lambda E, a=a, s=s: E.tensor_scalar(
                qvs[s][:], qu[:, a:a + 128], dxl[:], None, ALU.add),
                waits=[qvs_rdr[s], t_dxl])
        for s in range(4):
            a = 512 * b2 + 128 * s
            evacs = []
            for j in range(KC):
                c0 = T - 512 * (j + 1)
                bank = 4 if j % 2 == 0 else 5
                mmbd = p.op("tensor", lambda E, s=s, c0=c0, bank=bank: E.matmul(
                    ps[bank][:, 0:512], qvs[s][:], xlr[:, c0:c0 + 512],
                    start=True, stop=True),
                    waits=[tqv[s]] + bk.wr(bank))
                if j == KC - 1:
                    qvs_rdr[s] = mmbd
                tev = p.op("scalar", lambda E, j=j, W=W, bank=bank: E.copy(
                    bdraw[0][:, W - 512 * (j + 1):W - 512 * j], ps[bank][:, 0:512]),
                    waits=[mmbd, t_diagp[0]])
                bk.rd(bank, tev)
                evacs.append(tev)
            t_diag[s] = p.dma("sync", band[s][:, 0:W].bitcast(F32),
                              AP(bdraw[0].ap().tensor, W - 1 - a, [[BDP - 1, 128], [1, W]]),
                              waits=evacs + [band_rdr[s]], ch="bdd0")
            t_diagp[0] = t_diag[s]
        for s in range(4):
            a = 512 * b2 + 128 * s
            tas_hist = {}
            for kc in range(KC):
                bank = 6 if kc % 2 == 0 else 7
                j = kc % 4
                sl = slice(kc * 512, (kc + 1) * 512)
                mmac = p.op("tensor", lambda E, a=a, sl=sl, bank=bank: E.matmul(
                    ps[bank][:, 0:512], qu[:, a:a + 128], kfm[:, sl],
                    start=True, stop=True),
                    waits=bk.wr(bank))
                tac = p.op("scalar", lambda E, j=j, bank=bank: E.copy(
                    acst(j), ps[bank][:, 0:512]),
                    waits=[mmac, acst_rdr[j]])
                bk.rd(bank, tac)
                tas = p.op("vector", lambda E, s=s, sl=sl, j=j: E.tensor_tensor(
                    band[s][:, sl], acst(j), band[s][:, sl], ALU.add),
                    waits=[tac, t_diag[s]])
                acst_rdr[j] = tas
                if kc == KC - 1:
                    off = a % 512
                    tas = p.op("vector", lambda E, s=s, sl=sl, off=off: E.tensor_tensor(
                        band[s][:, sl], band[s][:, sl],
                        hmaskt[:, 512 - off:1024 - off], ALU.add),
                        waits=[tas, t_cst])
                tas_hist[kc] = tas
            tmx = p.op("vector", lambda E, s=s, W=W: E.reduce_max(
                mx[s][:], band[s][:, 0:W], AX.X),
                waits=[tas_hist[kc] for kc in range(KC)] + [mx_rdr[s]])
            tneg = p.op("vector", lambda E, s=s: E.tensor_scalar(nmx[s][:], mx[s][:], -1.0, None, ALU.mult),
                        waits=[tmx, exp_last[s]])
            mx_rdr[s] = tneg
            te = p.op("scalar", lambda E, s=s, W=W: E.activation(
                band[s][:, 0:W], band[s][:, 0:W], ACTF.Exp, bias=nmx[s][:],
                scale=1.0, accum_out=rsum[s][:]),
                waits=[tneg, rs_prev[s]] + [tas_hist[kc] for kc in range(KC)])
            for kc in range(KC):
                t_tp_ev[("exp", s, kc)] = te
            exp_last[s] = te
            t_tp_ev[("rs", s)] = p.op("vector", lambda E, s=s: E.reciprocal(risum[s][:], rsum[s][:]),
                                      waits=[te, ev_last[s]])

    def seed_acst_rdr():
        tok = t_gate_hist[-1] if t_gate_hist else None
        for j in range(4):
            acst_rdr[j] = acst_rdr[j] or tok

    def emit_tp(b2):
        for s in range(4):
            NKT = 4 * b2 + s + 1
            for g in range((NKT + 3) // 4):
                k0 = 4 * g
                gw = min(4, NKT - k0)
                bank = 6 + (g % 2)
                ttp = None
                for u in range(gw):
                    kt = k0 + u
                    ttp = p.op("tensor", lambda E, s=s, kt=kt, u=u, bank=bank: E.transpose(
                        ps[bank][:, u * 128:(u + 1) * 128].bitcast(F32R),
                        band[s][:, kt * 128:(kt + 1) * 128], identr[:]),
                        waits=[t_tp_ev[("exp", s, g)], t_idr] + (bk.wr(bank) if u == 0 else []))
                tev2 = p.op("scalar", lambda E, s=s, k0=k0, gw=gw, bank=bank: E.copy(
                    band[s][:, k0 * 128:(k0 + gw) * 128],
                    ps[bank][:, 0:gw * 128].bitcast(F32R)),
                    waits=[ttp])
                bk.rd(bank, tev2)
                for u in range(gw):
                    t_tp_ev[(s, k0 + u)] = tev2

    for b in range(NB):
        if b == 0:
            emit_scores(0)
            emit_tp(0)
        av_stop = {}
        t_avev = {}
        for q in range(4):
            for kt in range(4 * b + 4):
                i = kt % 3
                t_vb[i] = p.dma("sync", vbuf[i][:],
                                v_hbm.ap()[kt][:, q * 512:(q + 1) * 512],
                                waits=[t_vb[i], vb_rdr[i], t_vst[0], t_vst[1]], ch=f"vb{i}")
                last = None
                for s in range(4):
                    if kt > 4 * b + s:
                        continue
                    last = p.op("tensor", lambda E, s=s, kt=kt, i=i, b=b: E.matmul(
                        ps[s][:, 0:512],
                        band[s][:, kt * 128:(kt + 1) * 128],
                        vbuf[i][:],
                        start=(kt == 0), stop=(kt == 4 * b + s)),
                        waits=[t_vb[i], t_tp_ev[(s, kt)]] + (bk.wr(s) if kt == 0 else []))
                    if kt == 4 * b + s:
                        av_stop[(s, q)] = last
                vb_rdr[i] = last
            for s in range(4):
                ev = p.op("scalar", lambda E, s=s, q=q: E.activation(
                    wvt[s][:, q * 512:(q + 1) * 512], ps[s][:, 0:512],
                    ACTF.Identity, bias=0.0, scale=risum[s][:]),
                    waits=[av_stop[(s, q)], t_tp_ev[("rs", s)],
                           sq_rdr if (s == 0 and q < 2) else None])
                bk.rd(s, ev)
                t_avev[(s, q)] = ev
                if q == 3:
                    ev_last[s] = ev
        for s in range(4):
            band_rdr[s] = av_stop[(s, 3)]
            rs_prev[s] = t_tp_ev[("rs", s)]
        tg_tok = {}
        for s in range(4):
            for gh in range(2):
                ig = (s * 2 + gh) % 2
                t_gb[ig] = p.dma("sync", gbuf[ig][:],
                                 g_hbm.ap()[b * 4 + s][:, gh * 1024:(gh + 1) * 1024],
                                 waits=[t_gb[ig], t_gate_hist[-2] if len(t_gate_hist) >= 2 else None,
                                        acst_rdr[2 * ig], acst_rdr[2 * ig + 1],
                                        t_gst[0], t_gst[1]], ch=f"gb{ig}")
                tsi = p.op("scalar", lambda E, ig=ig: E.activation(gbuf[ig][:], gbuf[ig][:], ACTF.Silu),
                           waits=[t_gb[ig]])
                tg = p.op("vector", lambda E, s=s, gh=gh, ig=ig: E.tensor_mul(
                    wvt[s][:, gh * 1024:(gh + 1) * 1024],
                    wvt[s][:, gh * 1024:(gh + 1) * 1024], gbuf[ig][:]),
                    waits=[tsi, t_avev[(s, 2 * gh)], t_avev[(s, 2 * gh + 1)]])
                tg_tok[(s, gh)] = tg
                t_gate_hist.append(tg)
        t_wv_tp = {}
        for s in range(4):
            for g in range(4):
                bank = 4 + (g % 2)
                ttp = None
                for u in range(4):
                    vfc = 4 * g + u
                    ttp = p.op("tensor", lambda E, s=s, vfc=vfc, u=u, bank=bank: E.transpose(
                        ps[bank][:, u * 128:(u + 1) * 128].bitcast(F32R),
                        wvt[s][:, vfc * 128:(vfc + 1) * 128], identr[:]),
                        waits=[tg_tok[(s, g // 2)], t_idr] + (bk.wr(bank) if u == 0 else []))
                tev3 = p.op("scalar", lambda E, s=s, g=g, bank=bank: E.copy(
                    wvt[s][:, g * 512:(g + 1) * 512], ps[bank][:, 0:512].bitcast(F32R)),
                    waits=[ttp])
                bk.rd(bank, tev3)
                for u in range(4):
                    t_wv_tp[(s, 4 * g + u)] = tev3
        if b + 1 < NB:
            emit_scores(b + 1)
        t_xr = [None] * 4
        for s in range(4):
            tt = b * 4 + s
            t_xr[s] = p.dma("sync", ln1t[s][:], x_in[tt * 128:(tt + 1) * 128, :],
                            waits=ln1t_rdr[s], ch=f"xr{s}")
        o_stop = {}
        ln_add = {}
        for c in range(2):
            for vfc in range(VFC):
                jw = (c * VFC + vfc) % 2
                t_wob[jw] = p.dma("sync", wobuf[jw][:],
                                  wo_in[vfc * 128:(vfc + 1) * 128, c * 512:(c + 1) * 512],
                                  waits=[t_wob[jw], wo_rdr[jw]], ch=f"wo{jw}")
                last = None
                for s in range(4):
                    last = p.op("tensor", lambda E, s=s, vfc=vfc, jw=jw: E.matmul(
                        ps[s][:, 0:512],
                        wvt[s][:, vfc * 128:(vfc + 1) * 128],
                        wobuf[jw][:],
                        start=(vfc == 0), stop=(vfc == VFC - 1)),
                        waits=[t_wob[jw], t_wv_tp[(s, vfc)]] + (bk.wr(s) if vfc == 0 else []))
                    if vfc == VFC - 1:
                        o_stop[(s, c)] = last
                wo_rdr[jw] = last
            for s in range(4):
                ta_ = p.op("vector", lambda E, s=s, c=c: E.tensor_add(
                    ln1t[s][:, c * 512:(c + 1) * 512], ps[s][:, 0:512],
                    ln1t[s][:, c * 512:(c + 1) * 512]),
                    waits=[o_stop[(s, c)], t_xr[s]])
                bk.rd(s, ta_)
                ln_add[(s, c)] = ta_
        if b + 1 < NB:
            emit_tp(b + 1)
        for s in range(4):
            tt = b * 4 + s
            tsm = p.op("vector", lambda E, s=s: E.reduce_sum(smean[:], ln1t[s][:], AX.X),
                       waits=[ln_add[(s, 0)], ln_add[(s, 1)], t_srstd_rdr])
            tsq = p.op("scalar", lambda E, s=s: E.activation(
                wvt[0][:, 0:1024], ln1t[s][:], ACTF.Square, accum_out=smsq[:]),
                waits=[ln_add[(s, 0)], ln_add[(s, 1)], t_sqt_rdr])
            sq_rdr = tsq
            tmn = p.op("vector", lambda E: E.tensor_scalar(smean[:], smean[:], 1.0 / 1024.0, None, ALU.mult),
                       waits=[tsq, tsm])
            tms = p.op("vector", lambda E: E.tensor_scalar(smsq[:], smsq[:], 1.0 / 1024.0, None, ALU.mult),
                       waits=[tmn])
            tm2 = p.op("vector", lambda E: E.tensor_mul(svar[:], smean[:], smean[:]), waits=[tms])
            tvr = p.op("vector", lambda E: E.tensor_sub(svar[:], smsq[:], svar[:]), waits=[tm2])
            t_sqt_rdr = tvr
            tsd = p.op("scalar", lambda E: E.activation(srstd[:], svar[:], ACTF.Sqrt, bias=eps5[:]),
                       waits=[tvr, t_srstd_rdr, t_c5])
            trs2 = p.op("vector", lambda E: E.reciprocal(srstd[:], srstd[:]), waits=[tsd])
            tnorm = p.op("vector", lambda E, s=s: E.tensor_scalar(
                ln1t[s][:], ln1t[s][:], smean[:], srstd[:], ALU.subtract, ALU.mult),
                waits=[trs2])
            t_srstd_rdr = tnorm
            trm = p.op("vector", lambda E, s=s: E.reduce_max(
                tredA[:], ln1t[s][:], AX.X, apply_absolute_value=True), waits=[tnorm, t_lnmax])
            t_lnmax = p.op("vector", lambda E: E.tensor_max(lnmax[:], lnmax[:], tredA[:]),
                           waits=[trm])
            t_ln1 = p.dma("gpsimd", ln1_hbm.ap()[tt], ln1t[s][:], waits=[tnorm], ch=f"lnw{s}")
            ln1t_rdr[s] = [t_ln1, trm]
    t_ln1 = [t_ln1, t_ln1]

    def allreduce_max(tag, col_tok, col_tile, out_s, out_is, waits=()):
        cin, cout, csc = cc_bufs[tag]
        tw = p.dma("gpsimd", cin.ap().rearrange("(p o) -> p o", o=1), col_tile[:],
                   waits=[col_tok] + list(waits), ch="cc" + tag)
        tcc = p.op("gpsimd", lambda E: E.collective_compute(
            "AllReduce", ALU.max, replica_groups=[list(range(n_cores))],
            ins=[cin.ap()], outs=[cout.ap()]), waits=[tw])
        trb = p.dma("gpsimd", sc_row[:], cout.ap().rearrange("(o t) -> o t", o=1), waits=[tcc], ch="cc" + tag)
        u1 = p.op("vector", lambda E: E.reduce_max(sc_one[:], sc_row[:], AX.X), waits=[trb])
        u2 = p.op("vector", lambda E: E.tensor_mul(sc_tmp[:], sc_one[:], invqmax[:]), waits=[u1, t_invq])
        u3 = p.op("vector", lambda E: E.tensor_scalar(sc_tmp[:], sc_tmp[:], 1e-12, None, ALU.max),
                  waits=[u2])
        u4 = p.op("vector", lambda E: E.reciprocal(sc_one[:], sc_tmp[:]), waits=[u3])
        tws = p.dma("gpsimd", csc.ap()[0:1].rearrange("(p o) -> p o", o=1), sc_tmp[:],
                    waits=[u3, u4], ch="cc" + tag)
        twi = p.dma("gpsimd", csc.ap()[1:2].rearrange("(p o) -> p o", o=1), sc_one[:],
                    waits=[u4], ch="cc" + tag)
        b1e = p.dma("gpsimd", out_s[:], AP(csc, 0, [[0, 128], [1, 1]]),
                    waits=[twi], ch="cc" + tag)
        b2e = p.dma("gpsimd", out_is[:], AP(csc, 1, [[0, 128], [1, 1]]),
                    waits=[b1e], ch="cc" + tag)
        return [b1e, b2e]

    t_s1 = allreduce_max("a", t_lnmax, lnmax, scl["s1"], scl["is1"], waits=[t_ln1[0], t_ln1[1]])
    if "ln1_0" in dbg_outs:
        p.dma("sync", dbg_outs["ln1_0"], ln1_hbm.ap()[0], waits=[t_ln1[0], t_ln1[1]], ch="dbg")
        p.dma("sync", dbg_outs["ln1_31"], ln1_hbm.ap()[31], waits=[t_ln1[0], t_ln1[1]], ch="dbg")
        p.dma("sync", dbg_outs["s1"], scl["s1"][:], waits=t_s1, ch="dbg")
    if phases <= 2:
        return _finish()
    p.barrier()
    s2_stack.close()
    attn_stack.close()

    s4_stack = ExitStack()

    def sb4(name, shape, dt=F32):
        return s4_stack.enter_context(nc.sbuf_tensor(name, shape, dt))

    wslab = [sb4(f"wsl{i}", [128, D]) for i in range(2)]
    wqslab = [sb4(f"wql{i}", [128, D], F32R) for i in range(2)]
    wa = sb4("wa", [128, D])
    smax = sb4("smax", [128, 32])
    s_t = sb4("s_t", [128, 32])
    sinv = sb4("sinv", [128, 32])
    wstg = [sb4(f"wstg{i}", [128, 512], F32R) for i in range(2)]

    S = {"wa_rdr": None}
    slab_rdr = [None, None]
    wq_rdr = [None, None]
    stg_rdr = [None, None]

    def lpbq_load(idx, src_slice, in_cols, extra=()):
        i = idx % 2
        return p.dma("sync", wslab[i][:, 0:in_cols], src_slice,
                     waits=[slab_rdr[i]] + list(extra), ch=f"wsl{i}")

    def lpbq_dequant(idx, t_w, in_cols):
        i = idx % 2
        nb = in_cols // 32
        w3 = wslab[i][:, 0:in_cols].rearrange("p (b k) -> p b k", k=32)
        trd = p.op("vector", lambda E, w3=w3, nb=nb: E.reduce_max(
            smax[:, 0:nb], w3, AX.X, apply_absolute_value=True), waits=[t_w, S["wa_rdr"]])
        ts1 = p.op("vector", lambda E, nb=nb: E.tensor_scalar(
            s_t[:, 0:nb], smax[:, 0:nb], INV7_HI, None, ALU.mult), waits=[trd])
        ts2 = p.op("vector", lambda E, nb=nb: E.scalar_tensor_tensor(
            s_t[:, 0:nb], smax[:, 0:nb], INV7_LO, s_t[:, 0:nb], ALU.mult, ALU.add),
            waits=[ts1])
        ts3 = p.op("vector", lambda E, nb=nb: E.tensor_scalar(
            s_t[:, 0:nb], s_t[:, 0:nb], 1e-12, None, ALU.max), waits=[ts2])
        tsi = p.op("vector", lambda E, nb=nb: E.reciprocal(sinv[:, 0:nb], s_t[:, 0:nb]),
                   waits=[ts3])
        sinv_b = AP(sinv.ap().tensor, 0, [[32, 128], [1, nb], [0, 32]])
        st_b = AP(s_t.ap().tensor, 0, [[32, 128], [1, nb], [0, 32]])
        wa3 = wa[:, 0:in_cols].rearrange("p (b k) -> p b k", k=32)
        tm1 = p.op("vector", lambda E, w3=w3, sinv_b=sinv_b, wa3=wa3: E.tensor_tensor(
            wa3, w3, sinv_b, ALU.mult), waits=[tsi, S["wa_rdr"]])
        tm2 = p.op("vector", lambda E, in_cols=in_cols: E.tensor_scalar(
            wa[:, 0:in_cols], wa[:, 0:in_cols], biasA[:], None, ALU.add),
            waits=[tm1, t_biasA])
        tm3 = p.op("vector", lambda E, in_cols=in_cols: E.tensor_scalar(
            wa[:, 0:in_cols], wa[:, 0:in_cols], biasA[:], None, ALU.subtract),
            waits=[tm2])
        wq3 = wqslab[i][:, 0:in_cols].rearrange("p (b k) -> p b k", k=32)
        tm4 = p.op("vector", lambda E, wq3=wq3, wa3=wa3, st_b=st_b: E.tensor_tensor(
            wq3, wa3, st_b, ALU.mult), waits=[tm3, wq_rdr[i]])
        S["wa_rdr"] = tm4
        slab_rdr[i] = tm1
        return tm4

    lt_rdr = [None, None]
    lt_rdy = {}

    def emit_T(c, t_deq):
        i = c % 2
        bks = (4, 5) if i == 0 else (6, 7)
        last_tp = {}
        for dc in range(DC):
            bank = bks[dc // 4]
            ttp = p.op("tensor", lambda E, i=i, dc=dc, bank=bank: E.transpose(
                ps[bank][:, (dc % 4) * 128:((dc % 4) + 1) * 128].bitcast(F32R),
                wqslab[i][:, dc * 128:(dc + 1) * 128], identr[:]),
                waits=[t_deq, t_idr] + (bk.wr(bank) if dc % 4 == 0 else []))
            last_tp[bank] = ttp
            if dc == DC - 1:
                wq_rdr[i] = ttp
        evs = []
        for h, bank in enumerate(bks):
            tev = p.op("scalar", lambda E, i=i, h=h, bank=bank: E.copy(
                lt[i][:, h * 512:(h + 1) * 512], ps[bank][:, 0:512].bitcast(F32R)),
                waits=[last_tp[bank], lt_rdr[i]])
            bk.rd(bank, tev)
            evs.append(tev)
        lt_rdy[c] = evs

    chunks = ([("up", wup_in, upraw_hbm, ifc) for ifc in range(IFC)] +
              [("gate", wgate_in, gateraw_hbm, ifc) for ifc in range(IFC)])
    NCH = len(chunks)
    t_load = [None] * NCH

    def chunk_src(c):
        _, srcw, _, ifc = chunks[c]
        return srcw[ifc * 128:(ifc + 1) * 128, :]

    x1_stack = ExitStack()

    def sbx(name, shape, dt=F32):
        return x1_stack.enter_context(nc.sbuf_tensor(name, shape, dt))

    lt = [sbx(f"lt{i}", [128, D], F32R) for i in range(2)]
    rawst = [sbx(f"raw{i}", [128, T]) for i in range(2)]
    lnt = [sbx(f"lnt{i}", [128, D]) for i in range(2)]
    ast = [sbx(f"ast{i}", [128, 128]) for i in range(2)]
    x1 = [sbx(f"x1_{dc}", [128, T], F32R) for dc in range(DC)]

    t_load[0] = lpbq_load(0, chunk_src(0), D)
    t_load[1] = lpbq_load(1, chunk_src(1), D)
    emit_T(0, lpbq_dequant(0, t_load[0], D))
    lnt_rdr = [None, None]
    ast_rdr = [None, None]
    t_x1 = {}
    for tt in range(NT):
        i = tt % 2
        t_ln = p.dma("sync", lnt[i][:], ln1_hbm.ap()[tt], waits=[lnt_rdr[i], t_ln1[0], t_ln1[1]], ch=f"lnr{i}")
        last_tp = None
        for dc in range(DC):
            bank = 6 + (dc % 2)
            ttp = p.op("tensor", lambda E, i=i, dc=dc, bank=bank: E.transpose(
                ps[bank][:, 0:128], lnt[i][:, dc * 128:(dc + 1) * 128], identt[:]),
                waits=[t_ln] + bk.wr(bank))
            last_tp = ttp
            j = dc % 2
            ta = p.op("scalar", lambda E, j=j, bank=bank: E.activation(
                ast[j][:], ps[bank][:, 0:128], ACTF.Identity,
                bias=biasA[:], scale=scl["is1"][:]),
                waits=[ttp, t_biasA, ast_rdr[j]] + t_s1)
            bk.rd(bank, ta)
            tx = p.op("vector", lambda E, tt=tt, dc=dc, j=j: E.tensor_scalar(
                x1[dc][:, tt * 128:(tt + 1) * 128], ast[j][:], biasA[:], None, ALU.subtract),
                waits=[ta])
            ast_rdr[j] = tx
            t_x1[(tt, dc)] = tx
        lnt_rdr[i] = last_tp

    t_raw = [None, None]
    t_upmax = t_mx0["up"]
    t_gatemax = t_mx0["gate"]
    t_redprev = None
    for c in range(NCH):
        i = c % 2
        name, _, raw_h, ifc = chunks[c]
        if c + 2 < NCH:
            t_load[c + 2] = lpbq_load(c + 2, chunk_src(c + 2), D)
        if c + 1 < NCH:
            t_deq1 = lpbq_dequant(c + 1, t_load[c + 1], D)
        mm_stop = {}
        lastmm = None
        evs = []
        for half in range(2):
            for dc in range(DC):
                for q in range(4):
                    tc = half * 4 + q
                    tok = p.op("tensor", lambda E, i=i, dc=dc, q=q, tc=tc: E.matmul(
                        ps[q][:, 0:512], lt[i][:, dc * 128:(dc + 1) * 128],
                        x1[dc][:, tc * 512:(tc + 1) * 512],
                        start=(dc == 0), stop=(dc == DC - 1)),
                        waits=lt_rdy[c] +
                              [t_x1[(tc * 4 + k, dc)] for k in range(4)] +
                              (bk.wr(q) if dc == 0 else []))
                    if dc == DC - 1:
                        mm_stop[tc] = tok
                    lastmm = tok
            for q in range(4):
                tc = half * 4 + q
                ev = p.op("scalar", lambda E, i=i, tc=tc, q=q: E.activation(
                    rawst[i][:, tc * 512:(tc + 1) * 512], ps[q][:, 0:512],
                    ACTF.Identity, bias=0.0, scale=scl["s1"][:]),
                    waits=[mm_stop[tc], t_raw[i]] + t_s1)
                bk.rd(q, ev)
                evs.append(ev)
            if half == 0 and c + 1 < NCH:
                emit_T(c + 1, t_deq1)
        lt_rdr[i] = lastmm
        trd = p.op("vector", lambda E, i=i: E.reduce_max(
            tredA[:], rawst[i][:], AX.X, apply_absolute_value=True),
            waits=evs + [t_redprev])
        if name == "up":
            t_upmax = p.op("vector", lambda E: E.tensor_max(upmax[:], upmax[:], tredA[:]),
                           waits=[trd, t_upmax])
            t_redprev = t_upmax
        else:
            t_gatemax = p.op("vector", lambda E: E.tensor_max(gatemax[:], gatemax[:], tredA[:]),
                             waits=[trd, t_gatemax])
            t_redprev = t_gatemax
        t_raw[i] = p.dma("scalar", raw_h.ap()[ifc], rawst[i][:], waits=evs, ch=f"rww{i}")
    x1_stack.close()

    idx = NCH
    for oc in range(DC):
        for sc in range(3):
            t_w = lpbq_load(idx, wdown_in[oc * 128:(oc + 1) * 128, sc * 1024:(sc + 1) * 1024], D)
            tm4 = lpbq_dequant(idx, t_w, D)
            i = idx % 2
            bks = (4, 5) if i == 0 else (6, 7)
            last_tp = {}
            for dc in range(DC):
                bank = bks[dc // 4]
                ttp = p.op("tensor", lambda E, i=i, dc=dc, bank=bank: E.transpose(
                    ps[bank][:, (dc % 4) * 128:((dc % 4) + 1) * 128].bitcast(F32R),
                    wqslab[i][:, dc * 128:(dc + 1) * 128], identr[:]),
                    waits=[tm4, t_idr] + (bk.wr(bank) if dc % 4 == 0 else []))
                last_tp[bank] = ttp
                if dc == DC - 1:
                    wq_rdr[i] = ttp
            for h, bank in enumerate(bks):
                j = (idx * 2 + h) % 2
                tev = p.op("scalar", lambda E, j=j, bank=bank: E.copy(
                    wstg[j][:], ps[bank][:, 0:512].bitcast(F32R)),
                    waits=[last_tp[bank], stg_rdr[j]])
                bk.rd(bank, tev)
                dst = AP(wdownT_hbm, (sc * 8 + 4 * h) * 128 * D + oc * 128,
                         [[D, 128], [128 * D, 4], [1, 128]])
                stg_rdr[j] = p.dma("scalar", dst,
                                   wstg[j][:].rearrange("p (b k) -> p b k", k=128),
                                   waits=[tev], ch=f"wstg{j}")
            idx += 1
    t_wrow = [stg_rdr[0], stg_rdr[1]]

    t_s2 = allreduce_max("b1", t_upmax, upmax, scl["s2"], scl["is2"], waits=[t_raw[0], t_raw[1]])
    t_s3 = allreduce_max("b2", t_gatemax, gatemax, scl["s3"], scl["is3"], waits=t_s2)
    if "up0" in dbg_outs:
        p.dma("sync", dbg_outs["up0"], upraw_hbm.ap()[0], waits=[t_raw[0], t_raw[1]], ch="dbg")
        p.dma("sync", dbg_outs["gate0"], gateraw_hbm.ap()[0], waits=[t_raw[0], t_raw[1]], ch="dbg")
        p.dma("sync", dbg_outs["s2"], scl["s2"][:], waits=t_s2, ch="dbg")
        p.dma("sync", dbg_outs["s3"], scl["s3"][:], waits=t_s3, ch="dbg")
    if phases <= 4:
        return _finish()

    s4b_stack = ExitStack()

    def sb4b(name, shape, dt=F32):
        return s4b_stack.enter_context(nc.sbuf_tensor(name, shape, dt))

    gst = [sb4b(f"p2g{i}", [128, T]) for i in range(2)]
    ust = [sb4b(f"p3u{i}", [128, T]) for i in range(2)]
    aq = [sb4b(f"aq{i}", [128, T]) for i in range(2)]
    gq = [sb4b(f"gq{i}", [128, T]) for i in range(2)]
    sg = [sb4b(f"sg{i}", [128, T]) for i in range(2)]
    chain_hist = {}

    def gate_chain(i, waits):
        pw = [chain_hist.get(("v", i)), chain_hist.get(("s", i)), chain_hist.get(("tsp", i))]
        ta = p.op("scalar", lambda E, i=i: E.activation(
            aq[i][:], gst[i][:], ACTF.Identity, bias=biasA[:], scale=scl["is3"][:]),
            waits=list(waits) + t_s3 + [t_biasA] + pw)
        chain_hist[("gstrd", i)] = ta
        tg = p.op("vector", lambda E, i=i: E.tensor_scalar(
            gq[i][:], aq[i][:], biasA[:], scl["s3"][:], ALU.subtract, ALU.mult),
            waits=[ta] + pw)
        tsg = p.op("scalar", lambda E, i=i: E.activation(sg[i][:], gq[i][:], ACTF.Sigmoid), waits=[tg])
        tq1 = p.op("scalar", lambda E, i=i: E.activation(aq[i][:], sg[i][:], ACTF.Identity,
                                                         bias=biasA[:], scale=c64k[:]),
                   waits=[tsg, tg, t_c64k])
        tq2 = p.op("vector", lambda E, i=i: E.tensor_scalar(
            sg[i][:], aq[i][:], biasA[:], 65535.0, ALU.subtract, ALU.min), waits=[tq1])
        tt_ = p.op("vector", lambda E, i=i: E.scalar_tensor_tensor(
            sg[i][:], sg[i][:], 2.0 ** -16, gq[i][:], ALU.mult, ALU.mult), waits=[tq2])
        chain_hist[("s", i)] = tq1
        return tt_

    t_p2 = [None, None]
    t_sp = [None, None]
    t_tmax = t_mx0["t"]
    for ifc in range(IFC):
        i = ifc % 2
        t_p2[i] = p.dma("sync", gst[i][:], gateraw_hbm.ap()[ifc],
                        waits=[t_p2[i], chain_hist.get(("gstrd", i)), t_raw[0], t_raw[1]],
                        ch=f"g2r{i}")
        tt_ = gate_chain(i, [t_p2[i]])
        trd = p.op("vector", lambda E, i=i: E.reduce_max(
            tredA[:], sg[i][:], AX.X, apply_absolute_value=True), waits=[tt_, t_tmax])
        t_tmax = p.op("vector", lambda E: E.tensor_max(tmaxc[:], tmaxc[:], tredA[:]),
                      waits=[trd])
        chain_hist[("v", i)] = trd
        t_sp[i] = p.dma("scalar", tq_hbm.ap()[ifc], sg[i][:], waits=[tt_, t_sp[i]], ch=f"tw{i}")
        chain_hist[("tsp", i)] = t_sp[i]
    t_s4 = allreduce_max("c", t_tmax, tmaxc, scl["s4"], scl["is4"])

    t_p3g = [None, None]
    t_p3u = [None, None]
    t_p3o = [None, None]
    t_omax = t_mx0["o"]
    chain_hist.clear()
    for ifc in range(IFC):
        i = ifc % 2
        t_p3g[i] = p.dma("sync", gst[i][:], tq_hbm.ap()[ifc],
                         waits=[t_p3g[i], chain_hist.get(("v", i)), t_p3o[i], t_tmax,
                                t_sp[0], t_sp[1]],
                         ch=f"g2r{i}")
        t_p3u[i] = p.dma("sync", ust[i][:], upraw_hbm.ap()[ifc],
                         waits=[t_p3u[i], chain_hist.get(("ust", i)), t_tmax], ch=f"u3r{i}")
        ta2 = p.op("scalar", lambda E, i=i: E.activation(aq[i][:], gst[i][:], ACTF.Identity,
                                                         bias=biasA[:], scale=scl["is4"][:]),
                   waits=[t_p3g[i], chain_hist.get(("aqrd", i)), t_biasA] + t_s4)
        tg2 = p.op("vector", lambda E, i=i: E.tensor_scalar(
            sg[i][:], aq[i][:], biasA[:], scl["s4"][:], ALU.subtract, ALU.mult),
            waits=[ta2, chain_hist.get(("sgrd", i))])
        ta3 = p.op("scalar", lambda E, i=i: E.activation(aq[i][:], ust[i][:], ACTF.Identity,
                                                         bias=biasA[:], scale=scl["is2"][:]),
                   waits=[tg2, t_p3u[i]] + t_s2)
        chain_hist[("ust", i)] = ta3
        tu = p.op("vector", lambda E, i=i: E.tensor_scalar(
            gq[i][:], aq[i][:], biasA[:], scl["s2"][:], ALU.subtract, ALU.mult),
            waits=[ta3, chain_hist.get(("gqrd", i))])
        chain_hist[("aqrd", i)] = tu
        top = p.op("vector", lambda E, i=i: E.tensor_mul(gst[i][:], sg[i][:], gq[i][:]),
                   waits=[tu, tg2, t_p3o[i]])
        chain_hist[("sgrd", i)] = top
        chain_hist[("gqrd", i)] = top
        trd = p.op("vector", lambda E, i=i: E.reduce_max(
            tredA[:], gst[i][:], AX.X, apply_absolute_value=True), waits=[top, t_omax])
        t_omax = p.op("vector", lambda E: E.tensor_max(omaxc[:], omaxc[:], tredA[:]),
                      waits=[trd])
        chain_hist[("v", i)] = trd
        t_p3o[i] = p.dma("scalar", opre_hbm.ap()[ifc], gst[i][:], waits=[top], ch=f"opw{i}")
    t_s5 = allreduce_max("d", t_omax, omaxc, scl["s5"], scl["is5"], waits=[t_p3o[0], t_p3o[1]])
    if "op0" in dbg_outs:
        p.dma("sync", dbg_outs["op0"], opre_hbm.ap()[0], waits=[t_p3o[0], t_p3o[1]], ch="dbg")
        p.dma("sync", dbg_outs["s4"], scl["s4"][:], waits=t_s4, ch="dbg")
        p.dma("sync", dbg_outs["s5"], scl["s5"][:], waits=t_s5, ch="dbg")
    if phases <= 5:
        return _finish()
    p.barrier()
    s4b_stack.close()
    s4_stack.close()

    s5_stack = ExitStack()

    def sb5(name, shape, dt=F32):
        return s5_stack.enter_context(nc.sbuf_tensor(name, shape, dt))

    wdt = [sb5(f"wdt{ifc}", [128, D], F32R) for ifc in range(IFC)]
    t_wdt = [None] * IFC
    for ifc in range(IFC):
        t_wdt[ifc] = p.dma("sync", wdt[ifc][:], wdownT_hbm.ap()[ifc],
                           waits=[t_wrow[0], t_wrow[1]], ch=f"wdtg{ifc % 4}")
    for ifc in range(IFC):
        t_wdt[ifc] = t_wdt[IFC - 4 + (ifc % 4)]
    ob = [sb5(f"ob{i}", [128, 256]) for i in range(4)]
    rb = [sb5(f"rb{i}", [128, 256], F32R) for i in range(4)]
    lnt2 = [sb5(f"lnt2{i}", [128, D]) for i in range(2)]
    yt = [sb5(f"yt{i}", [128, D]) for i in range(2)]
    sqt2 = sb5("sqt2", [128, D])
    m_a = sb5("m_a", [128, 1])
    m_b = sb5("m_b", [128, 1])
    m_c = sb5("m_c", [128, 1])
    m_d = sb5("m_d", [128, 1])
    t_ob = [None] * 4
    t_rb = [None] * 4
    rb_rdr = [None] * 4
    t_yt = [None, None]
    t_ln2 = [None, None]
    t_sq2_rdr = None
    t_md_rdr = None
    ln2_rdr = [None, None]
    for tt in range(0, NT, 2):
        u = tt // 2
        iy = u % 2
        mm_stop = {}
        for k in range(2):
            t_ln2[k] = p.dma("gpsimd", lnt2[k][:], ln1_hbm.ap()[tt + k],
                             waits=[t_ln2[k], ln2_rdr[k]], ch=f"l2r{k}")
        for ifc in range(IFC):
            i = ifc % 4
            t_ob[i] = p.dma("sync", ob[i][:],
                            AP(opre_hbm, ifc * 128 * T + tt * 128, [[T, 128], [1, 256]]),
                            waits=[t_rb[i], rb_rdr[i], t_p3o[0], t_p3o[1]], ch=f"ob{i}")
            ta = p.op("scalar", lambda E, i=i: E.activation(
                ob[i][:], ob[i][:], ACTF.Identity, bias=biasA[:], scale=scl["is5"][:]),
                waits=[t_ob[i]] + t_s5)
            t_rb[i] = p.op("vector", lambda E, i=i: E.tensor_scalar(
                rb[i][:], ob[i][:], biasA[:], None, ALU.subtract),
                waits=[ta, rb_rdr[i]])
            last = None
            for k in range(2):
                for c in range(2):
                    bank = 4 * iy + 2 * k + c
                    last = p.op("tensor", lambda E, i=i, k=k, c=c, ifc=ifc, bank=bank: E.matmul(
                        ps[bank][:, 0:512], rb[i][:, k * 128:(k + 1) * 128],
                        wdt[ifc][:, c * 512:(c + 1) * 512],
                        start=(ifc == 0), stop=(ifc == IFC - 1)),
                        waits=[t_rb[i], t_wdt[ifc]] + (bk.wr(bank) if ifc == 0 else []))
                    if ifc == IFC - 1:
                        mm_stop[(k, c)] = last
            rb_rdr[i] = last
        for k in range(2):
            ty = tt + k
            tadds = []
            for c in range(2):
                bank = 4 * iy + 2 * k + c
                ta_ = p.op("vector", lambda E, k=k, c=c, bank=bank: E.scalar_tensor_tensor(
                    yt[k][:, c * 512:(c + 1) * 512], ps[bank][:, 0:512], scl["s5"][:],
                    lnt2[k][:, c * 512:(c + 1) * 512], ALU.mult, ALU.add),
                    waits=[mm_stop[(k, c)], t_ln2[k], t_yt[k]] + t_s5)
                bk.rd(bank, ta_)
                tadds.append(ta_)
                ln2_rdr[k] = ta_
            tsm = p.op("vector", lambda E, k=k: E.reduce_sum(m_a[:], yt[k][:], AX.X), waits=tadds)
            tsq = p.op("scalar", lambda E, k=k: E.activation(
                sqt2[:], yt[k][:], ACTF.Square, accum_out=m_b[:]),
                waits=tadds + [t_sq2_rdr])
            tmn = p.op("vector", lambda E: E.tensor_scalar(m_a[:], m_a[:], 1.0 / 1024.0, None, ALU.mult),
                       waits=[tsq, tsm])
            tms = p.op("vector", lambda E: E.tensor_scalar(m_b[:], m_b[:], 1.0 / 1024.0, None, ALU.mult),
                       waits=[tmn])
            tm2 = p.op("vector", lambda E: E.tensor_mul(m_c[:], m_a[:], m_a[:]), waits=[tms])
            tvr = p.op("vector", lambda E: E.tensor_sub(m_c[:], m_b[:], m_c[:]), waits=[tm2])
            t_sq2_rdr = tvr
            tsd = p.op("scalar", lambda E: E.activation(m_d[:], m_c[:], ACTF.Sqrt, bias=eps5[:]),
                       waits=[tvr, t_md_rdr, t_c5])
            trs = p.op("vector", lambda E: E.reciprocal(m_d[:], m_d[:]), waits=[tsd])
            tnorm = p.op("vector", lambda E, k=k: E.tensor_scalar(
                yt[k][:], yt[k][:], m_a[:], m_d[:], ALU.subtract, ALU.mult), waits=[trs])
            t_md_rdr = tnorm
            t_yt[k] = p.dma("gpsimd", y_out[ty * 128:(ty + 1) * 128, :], yt[k][:],
                            waits=[tnorm], ch=f"yw{k}")

    final_waits = [(k, v) for k, v in p.cnt.items() if k.startswith("D:") and v > 0]
    for e in ("vector", "scalar", "tensor"):
        final_waits.append(p.fence(e))
    p.waits_only("gpsimd", final_waits)

    with nc.Block() as block:
        p.build_block(block)
    s5_stack.close()
    g_stack.close()
    return nc


_CONSTS = None


def _in_maps(inputs, n=8):
    global _CONSTS
    if _CONSTS is None:
        _CONSTS = _host_constants()
    embT, hmask, ident = _CONSTS
    x = np.ascontiguousarray(np.asarray(inputs["L0_attn_x"], dtype=np.float32))
    common = dict(
        g_ln=np.ascontiguousarray(inputs["L0_attn_g_ln"], dtype=np.float32),
        wq=np.ascontiguousarray(inputs["L0_attn_W_q"], dtype=np.float32),
        wk=np.ascontiguousarray(inputs["L0_attn_W_k"], dtype=np.float32),
        wv=np.ascontiguousarray(inputs["L0_attn_W_v"], dtype=np.float32),
        wg=np.ascontiguousarray(inputs["L0_attn_W_g"], dtype=np.float32),
        wr=np.ascontiguousarray(inputs["L0_attn_W_r"], dtype=np.float32),
        xlu=np.ascontiguousarray(inputs["L0_attn_xl_u"], dtype=np.float32),
        xlv=np.ascontiguousarray(inputs["L0_attn_xl_v"], dtype=np.float32),
        wo=np.ascontiguousarray(inputs["L0_attn_W_o"], dtype=np.float32),
        wup=np.ascontiguousarray(inputs["L0_ffn_w_up"], dtype=np.float32),
        wgate=np.ascontiguousarray(inputs["L0_ffn_w_gate"], dtype=np.float32),
        wdown=np.ascontiguousarray(inputs["L0_ffn_w_down"], dtype=np.float32),
        embT=embT, hmask=hmask, ident=ident,
    )
    return [dict(common, x=x[c]) for c in range(n)]


def kernel(L0_attn_x, L0_attn_g_ln, L0_attn_W_q, L0_attn_W_k, L0_attn_W_v,
           L0_attn_W_g, L0_attn_W_r, L0_attn_xl_u, L0_attn_xl_v, L0_attn_W_o,
           L0_ffn_w_gate, L0_ffn_w_up, L0_ffn_w_down):
    n = 8
    in_maps = _in_maps(dict(
        L0_attn_x=L0_attn_x, L0_attn_g_ln=L0_attn_g_ln, L0_attn_W_q=L0_attn_W_q,
        L0_attn_W_k=L0_attn_W_k, L0_attn_W_v=L0_attn_W_v, L0_attn_W_g=L0_attn_W_g,
        L0_attn_W_r=L0_attn_W_r, L0_attn_xl_u=L0_attn_xl_u, L0_attn_xl_v=L0_attn_xl_v,
        L0_attn_W_o=L0_attn_W_o, L0_ffn_w_gate=L0_ffn_w_gate, L0_ffn_w_up=L0_ffn_w_up,
        L0_ffn_w_down=L0_ffn_w_down), n)
    nc = build(n_cores=n)
    res = run_bass_kernel_spmd(nc, in_maps, core_ids=list(range(n)))
    return np.stack([r["y"] for r in res.results]).astype(np.float32)


# version 24
# speedup vs baseline: 1.0052x; 1.0052x over previous
import math
from contextlib import ExitStack

import numpy as np

import concourse.bass as bass
import concourse.mybir as mybir
from concourse.bass import AP
from concourse.bass_utils import run_bass_kernel_spmd

F32 = mybir.dt.float32
F32R = mybir.dt.float32r
ALU = mybir.AluOpType
ACTF = mybir.ActivationFunctionType
AX = mybir.AxisListType

T = 4096
D = 1024
KD = 128
V = 2048
I = 3072
NB = 8
NT = 32
DC = 8
IFC = 24
VFC = 16

M23 = 1.5 * 2.0 ** 23
M52 = 1.5 * 2.0 ** 52
_IT = 1.0 / math.sqrt(128.0)
INV_TAU_HI = float(np.float32(_IT))
INV_TAU_LO = float(np.float32(_IT - INV_TAU_HI))
_I7 = 1.0 / 7.0
INV7_HI = float(np.float32(_I7))
INV7_LO = float(np.float32(_I7 - INV7_HI))
_IQ = 1.0 / 32767.0
INV_QMAX_HI = float(np.float32(_IQ))
INV_QMAX_LO = float(np.float32(_IQ - INV_QMAX_HI))

ENGS = ("sync", "vector", "scalar", "tensor", "gpsimd")
DMA_ENGS = ("sync", "scalar", "gpsimd")


def _host_constants():
    try:
        import jax
        import jax.numpy as jnp
        cpu = jax.devices("cpu")[0]
        with jax.default_device(cpu):
            pos = jnp.arange(T, dtype=jnp.float32)
            inv_lams = 1.0 / 10000.0 ** (jnp.arange(0, D, 2, dtype=jnp.float32) / D)
            pre = pos[:, None] * inv_lams[None, :]
            cat = jnp.concatenate([jnp.sin(pre), jnp.cos(pre)], axis=-1)
            emb = jnp.flip(cat, axis=0)
            embT = np.ascontiguousarray(np.asarray(emb).T.astype(np.float32))
    except Exception:
        pos = np.arange(T, dtype=np.float32)
        expo = (np.arange(0, D, 2).astype(np.float32) / np.float32(D)).astype(np.float32)
        inv_lams = (np.float32(1.0) / np.power(np.float32(10000.0), expo)).astype(np.float32)
        pre = (pos[:, None] * inv_lams[None, :]).astype(np.float32)
        cat = np.concatenate([np.sin(pre).astype(np.float32), np.cos(pre).astype(np.float32)], axis=1)
        embT = np.ascontiguousarray(cat[::-1].T.astype(np.float32))
    u = np.arange(1024)[None, :]
    pp = np.arange(128)[:, None]
    hmask = np.where(u <= pp + 512, np.float32(0.0), np.float32(-1e30)).astype(np.float32)
    ident = np.eye(128, dtype=np.float32)
    return embT, hmask, ident


class Prog:
    def __init__(self, nc):
        self.nc = nc
        self.steps = {e: [] for e in ENGS}
        self.sems = {}
        self.cnt = {}
        for e in ENGS:
            self.sems[e] = nc.alloc_semaphore(f"S_{e}")
            self.cnt[e] = 0

    def op(self, eng, fn, waits=(), inc=True):
        tok = None
        if inc:
            self.cnt[eng] += 1
            tok = (eng, self.cnt[eng])
        ws = tuple(w for w in waits if w is not None)
        self.steps[eng].append((ws, fn, eng if inc else None))
        return tok

    def dma(self, eng, out_ap, in_ap, waits=(), ch="misc", **kw):
        key = "D:" + ch
        if key not in self.sems:
            self.sems[key] = self.nc.alloc_semaphore("SD_" + ch)
            self.cnt[key] = 0
        self.cnt[key] += 16
        tok = (key, self.cnt[key])
        ws = tuple(w for w in waits if w is not None)

        def fn(E, o=out_ap, i=in_ap, kw=kw):
            return E.dma_start(out=o, in_=i, **kw)

        self.steps[eng].append((ws, fn, key))
        return tok

    def fence(self, eng, waits=()):
        if self.cnt[eng] == 0:
            return None
        return (eng, self.cnt[eng])

    def waits_only(self, eng, waits):
        ws = tuple(w for w in waits if w is not None)
        if ws:
            self.steps[eng].append((ws, None, None))

    def barrier(self):
        toks = [self.fence(e) for e in ENGS]
        dtoks = [(k, v) for k, v in self.cnt.items() if k.startswith("D:") and v > 0]
        for e in ENGS:
            self.waits_only(e, [t for t in toks if t is not None and t[0] != e] + dtoks)

    def build_block(self, block):
        for e in ENGS:
            steps = self.steps[e]
            sems = self.sems

            def section(eng, steps=steps, sems=sems):
                seen = {}
                for ws, fn, inckey in steps:
                    for (k, v) in ws:
                        if seen.get(k, 0) >= v:
                            continue
                        eng.wait_ge(sems[k], v)
                        seen[k] = v
                    if fn is None:
                        continue
                    inst = fn(eng)
                    if inckey is not None:
                        inst.then_inc(sems[inckey], 16 if inckey.startswith("D:") else 1)

            getattr(block, e)(section)


class Banks:
    def __init__(self):
        self.cons = {i: [] for i in range(8)}

    def wr(self, i):
        w = self.cons[i]
        self.cons[i] = []
        return list(w)

    def rd(self, i, tok):
        if tok is not None:
            self.cons[i].append(tok)


def build(n_cores=8, dbg=(), phases=9):
    nc = bass.Bass("TRN2", target_bir_lowering=False, detect_race_conditions=False)
    p = Prog(nc)
    bk = Banks()

    x_in = nc.dram_tensor("x", [T, D], F32, kind="ExternalInput").ap()
    gln_in = nc.dram_tensor("g_ln", [D], F32, kind="ExternalInput").ap()
    wq_in = nc.dram_tensor("wq", [D, KD], F32R, kind="ExternalInput").ap()
    wk_in = nc.dram_tensor("wk", [D, KD], F32R, kind="ExternalInput").ap()
    wv_in = nc.dram_tensor("wv", [D, V], F32R, kind="ExternalInput").ap()
    wg_in = nc.dram_tensor("wg", [D, V], F32R, kind="ExternalInput").ap()
    wr_in = nc.dram_tensor("wr", [D, KD], F32R, kind="ExternalInput").ap()
    xlu_in = nc.dram_tensor("xlu", [KD], F32, kind="ExternalInput").ap()
    xlv_in = nc.dram_tensor("xlv", [KD], F32, kind="ExternalInput").ap()
    wo_in = nc.dram_tensor("wo", [V, D], F32R, kind="ExternalInput").ap()
    wup_in = nc.dram_tensor("wup", [I, D], F32, kind="ExternalInput").ap()
    wgate_in = nc.dram_tensor("wgate", [I, D], F32, kind="ExternalInput").ap()
    wdown_in = nc.dram_tensor("wdown", [D, I], F32, kind="ExternalInput").ap()
    embt_in = nc.dram_tensor("embT", [D, T], F32R, kind="ExternalInput").ap()
    hmask_in = nc.dram_tensor("hmask", [128, 1024], F32, kind="ExternalInput").ap()
    ident_in = nc.dram_tensor("ident", [128, 128], F32, kind="ExternalInput").ap()
    y_out = nc.dram_tensor("y", [T, D], F32, kind="ExternalOutput").ap()
    dbg_outs = {}
    for name, shape in dbg:
        dbg_outs[name] = nc.dram_tensor("dbg_" + name, list(shape), F32, kind="ExternalOutput").ap()

    v_hbm = nc.dram_tensor("v_hbm", [NT, 128, V], F32R, kind="Internal")
    g_hbm = nc.dram_tensor("g_hbm", [NT, 128, V], F32R, kind="Internal")
    bd_hbm = nc.dram_tensor("bd_hbm", [128 * T + 1024], F32, kind="Internal")
    ln1_hbm = nc.dram_tensor("ln1_hbm", [NT, 128, D], F32, kind="Internal")
    upraw_hbm = nc.dram_tensor("upraw_hbm", [IFC, 128, T], F32, kind="Internal")
    gateraw_hbm = nc.dram_tensor("gateraw_hbm", [IFC, 128, T], F32, kind="Internal")
    opre_hbm = nc.dram_tensor("opre_hbm", [IFC, 128, T], F32, kind="Internal")
    tq_hbm = nc.dram_tensor("tq_hbm", [IFC, 128, T], F32, kind="Internal")
    wupT_hbm = nc.dram_tensor("wupT_hbm", [DC, 128, I], F32R, kind="Internal")
    wgateT_hbm = nc.dram_tensor("wgateT_hbm", [DC, 128, I], F32R, kind="Internal")
    wdownT_hbm = nc.dram_tensor("wdownT_hbm", [IFC, 128, D], F32R, kind="Internal")
    cc_bufs = {}
    for tag in ("a", "b1", "b2", "c", "d"):
        cc_bufs[tag] = (
            nc.dram_tensor(f"cc_{tag}_in", [128], F32, kind="Internal"),
            nc.dram_tensor(f"cc_{tag}_out", [128], F32, kind="Internal"),
            nc.dram_tensor(f"cc_{tag}_sc", [2], F32, kind="Internal"),
        )


    def _finish():
        fw = [(k, v) for k, v in p.cnt.items() if k.startswith("D:") and v > 0]
        for e in ("vector", "scalar", "tensor"):
            fw.append(p.fence(e))
        p.waits_only("gpsimd", fw)
        with nc.Block() as block:
            p.build_block(block)
        return nc
    g_stack = ExitStack()

    def sb(name, shape, dt=F32):
        return g_stack.enter_context(nc.sbuf_tensor(name, shape, dt))

    identt = sb("identt", [128, 128])
    identr = sb("identr", [128, 128], F32R)
    gln = sb("gln", [128, DC])
    xlu = sb("xlu_t", [128, 1])
    xlv = sb("xlv_t", [128, 1])
    det = sb("det", [128, 1])
    ones1 = sb("ones1", [128, 1])
    biasA = sb("biasA", [128, 1])
    m1t = sb("m1t", [128, 1])
    m2t = sb("m2t", [128, 1])
    invtau = sb("invtau", [128, 1])
    invqmax = sb("invqmax", [1, 1])
    ones_row = sb("ones_row", [1, 128])
    scl = {}
    for tag in ("s1", "s2", "s3", "s4", "s5"):
        scl[tag] = sb(tag + "_b", [128, 1])
        scl["i" + tag] = sb("i" + tag + "_b", [128, 1])
    lnmax = sb("lnmax", [128, 1])
    upmax = sb("upmax", [128, 1])
    gatemax = sb("gatemax", [128, 1])
    tmaxc = sb("tmaxc", [128, 1])
    omaxc = sb("omaxc", [128, 1])
    sc_row = sb("sc_row", [1, 128])
    sc_one = sb("sc_one", [1, 1])
    sc_tmp = sb("sc_tmp", [1, 1])
    tredA = sb("tredA", [128, 1])
    tredE = sb("tredE", [128, 1])
    tredE2 = sb("tredE2", [128, 1])
    eps6 = sb("eps6", [128, 1])
    eps5 = sb("eps5", [128, 1])
    c64k = sb("c64k", [128, 1])
    c1k = sb("c1k", [128, 1])

    attn_stack = ExitStack()

    def sba(name, shape, dt=F32):
        return attn_stack.enter_context(nc.sbuf_tensor(name, shape, dt))

    hmaskt = sba("hmaskt", [128, 1024])
    qu = sba("qu", [128, T], F32R)
    kfm = sba("kfm", [128, T], F32R)
    xlr = sba("xlr", [128, T], F32R)

    s1_stack = ExitStack()

    def sb1(name, shape, dt=F32):
        return s1_stack.enter_context(nc.sbuf_tensor(name, shape, dt))

    p0_stack = ExitStack()

    def sb0(name, shape, dt=F32):
        return p0_stack.enter_context(nc.sbuf_tensor(name, shape, dt))

    wq_t = [sb1(f"wq{dc}", [128, KD], F32R) for dc in range(DC)]
    wk_t = [sb1(f"wk{dc}", [128, KD], F32R) for dc in range(DC)]
    wr_t = [sb1(f"wr{dc}", [128, KD], F32R) for dc in range(DC)]
    rr_col = sb1("rr_col", [128, NT])

    ps = [g_stack.enter_context(nc.psum_tensor(f"p{i}", [128, 512], F32)) for i in range(8)]

    import kernel as _K1
    if "cst1" not in getattr(_K1, "SKIP", set()):
        p.dma("sync", identt[:], ident_in, ch="cst")
    p.dma("sync", hmaskt[:], hmask_in, ch="cst")
    for dc in range(DC):
        p.dma("sync", gln[:, dc:dc + 1],
              gln_in[dc * 128:(dc + 1) * 128].rearrange("(p o) -> p o", o=1), ch="cst")
    p.dma("sync", xlu[:], xlu_in.rearrange("(p o) -> p o", o=1), ch="cst")
    p.dma("sync", xlv[:], xlv_in.rearrange("(p o) -> p o", o=1), ch="cst")

    import kernel as _K0
    _SKIP = getattr(_K0, "SKIP", set())
    t0 = p.op("vector", lambda E: E.memset(det[:], 2.0 ** -30))
    t1 = p.op("vector", lambda E: E.memset(ones1[:], 1.0))
    t2 = p.op("vector", lambda E: E.tensor_scalar(det[:], det[:], 1.0, None, ALU.add), waits=[t0])
    t3 = p.op("vector", lambda E: E.tensor_tensor(det[:], det[:], ones1[:], ALU.is_equal), waits=[t1, t2])
    t4 = p.op("vector", lambda E: E.tensor_scalar(m1t[:], det[:], M23, None, ALU.mult), waits=[t3])
    t5 = p.op("vector", lambda E: E.tensor_scalar(m2t[:], det[:], -M52, M52, ALU.mult, ALU.add), waits=[t3])
    t_biasA = p.op("vector", lambda E: E.tensor_add(biasA[:], m1t[:], m2t[:]), waits=[t4, t5])
    t6 = p.op("vector", lambda E: E.memset(invtau[:], INV_TAU_HI))
    t_invtau = p.op("vector", lambda E: E.tensor_scalar(invtau[:], invtau[:], INV_TAU_LO, None, ALU.add),
                    waits=[t6])
    t7 = p.op("vector", lambda E: E.memset(invqmax[:], INV_QMAX_HI))
    t_invq = p.op("vector", lambda E: E.tensor_scalar(invqmax[:], invqmax[:], INV_QMAX_LO, None, ALU.add),
                  waits=[t7])
    t_or = p.op("vector", lambda E: E.memset(ones_row[:], 1.0))
    t_c6 = p.op("vector", lambda E: E.memset(eps6[:], 1e-6))
    t_c5 = p.op("vector", lambda E: E.memset(eps5[:], 1e-5))
    t_c64k = p.op("vector", lambda E: E.memset(c64k[:], 65536.0))
    t_c1k = p.op("vector", lambda E: E.memset(c1k[:], 1.0 / 1024.0))
    t_mx0 = {}
    for tag, tile in (("ln", lnmax), ("up", upmax), ("gate", gatemax), ("t", tmaxc), ("o", omaxc)):
        t_mx0[tag] = p.op("vector", lambda E, tile=tile: E.memset(tile[:], 0.0))


    if phases <= -3:
        return _finish()
    xt_l = [sb0("p0x0", [128, D]), sb0("p0x1", [128, D])]
    sq_l = [sb0("p0sq0", [128, D]), sb0("p0sq1", [128, D])]
    ms_col = sb0("ms_col", [128, NT])
    t_sq = [None, None]
    t_ms = [None, None]
    t_ms_all = []
    for tt in range(NT if "rrms" not in _SKIP else 0):
        i = tt % 2
        t_x = p.dma("sync", xt_l[i][:], x_in[tt * 128:(tt + 1) * 128, :], waits=[t_ms[i]], ch=f"p0x{i}")
        t_sq[i] = p.op("scalar", lambda E, a=sq_l[i], b=xt_l[i]: E.activation(a[:], b[:], ACTF.Square),
                       waits=[t_x])
        t_ms[i] = p.op("vector", lambda E, o=ms_col[:, tt:tt + 1], a=sq_l[i]: E.reduce_sum(o, a[:], AX.X),
                       waits=[t_sq[i]])
        t_ms_all.append(t_ms[i])
    sd_col = sb0("sd_col", [128, NT])
    if "rrms" not in _SKIP:
        t_sd = p.op("scalar", lambda E: E.activation(sd_col[:], ms_col[:], ACTF.Sqrt,
                                                     bias=eps6[:], scale=c1k[:]),
                    waits=t_ms_all + [t_c6, t_c1k])
        t_rc = p.op("vector", lambda E: E.reciprocal(rr_col[:], sd_col[:]), waits=[t_sd])
    else:
        t_rc = None

    if phases <= -2:
        return _finish()
    for dc in range(DC):
        p.dma("sync", wq_t[dc][:], wq_in[dc * 128:(dc + 1) * 128, :], ch="cst")
        p.dma("sync", wk_t[dc][:], wk_in[dc * 128:(dc + 1) * 128, :], ch="cst")
        t_cst = p.dma("sync", wr_t[dc][:], wr_in[dc * 128:(dc + 1) * 128, :], ch="cst")

    t_idr = p.op("vector", lambda E: E.tensor_copy(identr[:], identt[:]), waits=[t_cst])
    emb_l = [sb0("emb0", [128, 512], F32R), sb0("emb1", [128, 512], F32R)]
    emb_rdr = [None, None]
    if phases <= -1:
        return _finish()
    import kernel as _K
    for pc in range(getattr(_K, "XLR_PCS", 8)):
        mm_last = None
        for dc in range(DC):
            i = (pc * DC + dc) % 2
            t_e = p.dma("sync", emb_l[i][:],
                        embt_in[dc * 128:(dc + 1) * 128, pc * 512:(pc + 1) * 512],
                        waits=[emb_rdr[i]], ch=f"emb{i}")
            mm_last = p.op("tensor", lambda E, dc=dc, i=i: E.matmul(
                ps[2][:, 0:512], wr_t[dc][:], emb_l[i][:],
                start=(dc == 0), stop=(dc == DC - 1)),
                waits=[t_e, t_cst] + (bk.wr(2) if dc == 0 else []))
            emb_rdr[i] = mm_last
        ev = p.op("vector", lambda E, pc=pc: E.tensor_copy(xlr[:, pc * 512:(pc + 1) * 512], ps[2][:, 0:512]),
                  waits=[mm_last])
        bk.rd(2, ev)

    if phases <= 0:
        return _finish()
    p.barrier()
    p0_stack.close()

    xband = [sb1(f"xb{i}", [128, D]) for i in range(4)]
    xtb = [sb1(f"xtb{dc}", [128, 512], F32R) for dc in range(DC)]
    wbig = [sb1(f"wbig{dc}", [128, V], F32R) for dc in range(DC)]
    vstage = [sb1(f"vst{i}", [128, V], F32R) for i in range(2)]
    gstage = vstage

    def emit_xt_build(b, f_tensor_prev):
        t_xb = []
        for ts in range(4):
            tt = b * 4 + ts
            td = p.dma("sync", xband[ts][:], x_in[tt * 128:(tt + 1) * 128, :],
                       waits=[f_tensor_prev], ch=f"xb{ts}")
            t_xb.append(p.op("vector", lambda E, ts=ts, tt=tt: E.tensor_scalar(
                xband[ts][:], xband[ts][:], rr_col[:, tt:tt + 1], None, ALU.mult),
                waits=[td, t_rc]))
        toks = {}
        for ts in range(4):
            for dc in range(DC):
                bank = 6 + ((ts * DC + dc) % 2)
                ttp = p.op("tensor", lambda E, ts=ts, dc=dc, bank=bank: E.transpose(
                    ps[bank][:, 0:128], xband[ts][:, dc * 128:(dc + 1) * 128], identt[:]),
                    waits=[t_xb[ts], t_cst] + bk.wr(bank))
                tev = p.op("scalar", lambda E, ts=ts, dc=dc, bank=bank: E.activation(
                    xtb[dc][:, ts * 128:(ts + 1) * 128], ps[bank][:, 0:128],
                    ACTF.Identity, bias=0.0, scale=gln[:, dc:dc + 1]),
                    waits=[ttp, t_cst])
                bk.rd(bank, tev)
                toks[(ts, dc)] = tev
        return toks

    t_wv = None
    for dc in range(DC):
        t_wv = p.dma("sync", wbig[dc][:], wv_in[dc * 128:(dc + 1) * 128, :], ch="wvload")
    f_tensor = None
    t_vst = [None, None]
    vst_evs = [[], []]
    for b in range(NB):
        t_xt = emit_xt_build(b, f_tensor)
        cols = slice(b * 512, (b + 1) * 512)
        mmq = mmk = None
        for dc in range(DC):
            w = [t_xt[(ts, dc)] for ts in range(4)]
            mmq = p.op("tensor", lambda E, dc=dc: E.matmul(
                ps[0][:, 0:512], wq_t[dc][:], xtb[dc][:],
                start=(dc == 0), stop=(dc == DC - 1)),
                waits=w + [t_cst] + (bk.wr(0) if dc == 0 else []))
            mmk = p.op("tensor", lambda E, dc=dc: E.matmul(
                ps[1][:, 0:512], wk_t[dc][:], xtb[dc][:],
                start=(dc == 0), stop=(dc == DC - 1)),
                waits=[t_cst] + (bk.wr(1) if dc == 0 else []))
        tq1 = p.op("vector", lambda E, cols=cols: E.tensor_scalar(
            qu[:, cols], ps[0][:, 0:512], invtau[:], xlu[:], ALU.mult, ALU.add),
            waits=[mmq, t_invtau, t_cst])
        bk.rd(0, tq1)
        tk1 = p.op("scalar", lambda E, cols=cols: E.copy(kfm[:, cols], ps[1][:, 0:512]), waits=[mmk])
        bk.rd(1, tk1)
        for ts in range(4):
            i = ts % 2
            evs = []
            for vc in range(4):
                bank = 2 + vc
                mmv = None
                for dc in range(DC):
                    mmv = p.op("tensor", lambda E, ts=ts, vc=vc, dc=dc, bank=bank: E.matmul(
                        ps[bank][:, 0:512], xtb[dc][:, ts * 128:(ts + 1) * 128],
                        wbig[dc][:, vc * 512:(vc + 1) * 512],
                        start=(dc == 0), stop=(dc == DC - 1)),
                        waits=[t_xt[(ts, dc)], t_wv] + (bk.wr(bank) if dc == 0 else []))
                ev = p.op("scalar", lambda E, i=i, vc=vc, bank=bank: E.copy(
                    vstage[i][:, vc * 512:(vc + 1) * 512], ps[bank][:, 0:512]),
                    waits=[mmv, t_vst[i]])
                bk.rd(bank, ev)
                evs.append(ev)
            t_vst[i] = p.dma("scalar", v_hbm.ap()[b * 4 + ts], vstage[i][:], waits=evs, ch=f"vst{i}")
            vst_evs[i] = evs
        f_tensor = p.fence("tensor")

    t_wg = None
    for dc in range(DC):
        t_wg = p.dma("sync", wbig[dc][:], wg_in[dc * 128:(dc + 1) * 128, :],
                     waits=[f_tensor], ch="wvload")
    t_gst = [t_vst[0], t_vst[1]]
    for b in range(NB):
        t_xt = emit_xt_build(b, f_tensor)
        for ts in range(4):
            i = ts % 2
            evs = []
            for vc in range(4):
                bank = 2 + vc
                mmg = None
                for dc in range(DC):
                    mmg = p.op("tensor", lambda E, ts=ts, vc=vc, dc=dc, bank=bank: E.matmul(
                        ps[bank][:, 0:512], xtb[dc][:, ts * 128:(ts + 1) * 128],
                        wbig[dc][:, vc * 512:(vc + 1) * 512],
                        start=(dc == 0), stop=(dc == DC - 1)),
                        waits=[t_xt[(ts, dc)], t_wg] + (bk.wr(bank) if dc == 0 else []))
                ev = p.op("scalar", lambda E, i=i, vc=vc, bank=bank: E.copy(
                    gstage[i][:, vc * 512:(vc + 1) * 512], ps[bank][:, 0:512]),
                    waits=[mmg, t_gst[i]])
                bk.rd(bank, ev)
                evs.append(ev)
            t_gst[i] = p.dma("scalar", g_hbm.ap()[b * 4 + ts], gstage[i][:], waits=evs, ch=f"vst{i}")
        f_tensor = p.fence("tensor")

    if phases <= 1:
        if "qu" in dbg_outs:
            p.dma("gpsimd", dbg_outs["qu"], qu[:], waits=[p.fence("vector")], ch="dbg")
            p.dma("gpsimd", dbg_outs["kfm"], kfm[:], waits=[p.fence("scalar")], ch="dbg")
            p.dma("gpsimd", dbg_outs["xlr"], xlr[:], waits=[], ch="dbg")
            p.dma("gpsimd", dbg_outs["v0"], v_hbm.ap()[0], waits=[t_vst[0], t_vst[1]], ch="dbg")
            p.dma("gpsimd", dbg_outs["g0"], g_hbm.ap()[0], waits=[t_gst[0], t_gst[1]], ch="dbg")
        return _finish()

    p.barrier()
    s1_stack.close()

    s2_stack = ExitStack()

    def sb2(name, shape, dt=F32):
        return s2_stack.enter_context(nc.sbuf_tensor(name, shape, dt))

    band = [sb2(f"band{s}", [128, T], F32R) for s in range(4)]
    vbuf = [sb2(f"vb{i}", [128, 512], F32R) for i in range(3)]
    wvt = [sb2(f"wvg{s}", [128, V], F32R) for s in range(4)]
    gbuf = [sb2(f"gb{i}", [128, 1024], F32R) for i in range(2)]
    wobuf = [sb2(f"wob{i}", [128, 512], F32R) for i in range(2)]
    BDP = 4608
    bdraw = [sb2("bdraw0", [128, BDP])]
    ln1t = [sb2(f"ln1t{s}", [128, D]) for s in range(4)]
    qvs = [sb2(f"qvs{i}", [128, 128], F32R) for i in range(4)]
    dxl = sb2("dxl", [128, 1])
    mx = [sb2(f"mx{s}", [128, 1]) for s in range(4)]
    nmx = [sb2(f"nmx{s}", [128, 1]) for s in range(4)]
    rsum = [sb2(f"rsum{s}", [128, 1]) for s in range(4)]
    risum = [sb2(f"risum{s}", [128, 1]) for s in range(4)]
    smean = sb2("smean", [128, 1])
    smsq = sb2("smsq", [128, 1])
    svar = sb2("svar", [128, 1])
    srstd = sb2("srstd", [128, 1])

    t_bdz0 = p.op("vector", lambda E: E.memset(bdraw[0][:], 0.0))
    t_dxl = p.op("vector", lambda E: E.tensor_sub(dxl[:], xlv[:], xlu[:]), waits=[t_cst])
    SD = {"diagp": [t_bdz0]}
    t_diag = {}
    vb_rdr = [None] * 4
    band_rdr = [None] * 4
    qvs_rdr = [None] * 4
    S_exp = [None, None]
    t_vb = [None] * 4
    t_gb = [None, None]
    t_wob = [None] * 4
    t_ln1 = None
    t_lnmax = t_mx0["ln"]
    t_sqt_rdr = None
    t_srstd_rdr = None
    t_gate_hist = []
    tsum = None
    t_tp_ev = {}
    exp_last = [None] * 4
    mx_rdr = [None] * 4
    acst_rdr = [None] * 4
    rs_prev = [None] * 4
    ev_last = [None] * 4
    sq_rdr = None
    ln1t_rdr = [[] for _ in range(4)]
    wo_rdr = [None] * 4

    def emit_scores(b2):
        KC = b2 + 1
        W = 512 * KC

        def acst(j):
            return gbuf[j // 2][:, (j % 2) * 512:((j % 2) + 1) * 512]
        gh_tok = t_gate_hist[-1] if t_gate_hist else None
        for j in range(4):
            if acst_rdr[j] is None:
                acst_rdr[j] = gh_tok
            else:
                acst_rdr[j] = acst_rdr[j]
        tqv = [None] * 4
        for s in range(4):
            a = 512 * b2 + 128 * s
            tqv[s] = p.op("vector", lambda E, a=a, s=s: E.tensor_scalar(
                qvs[s][:], qu[:, a:a + 128], dxl[:], None, ALU.add),
                waits=[qvs_rdr[s], t_dxl])
        for s in range(4):
            a = 512 * b2 + 128 * s
            evacs = []
            for j in range(KC):
                c0 = T - 512 * (j + 1)
                bank = 4 + (j % 4)
                mmbd = p.op("tensor", lambda E, s=s, c0=c0, bank=bank: E.matmul(
                    ps[bank][:, 0:512], qvs[s][:], xlr[:, c0:c0 + 512],
                    start=True, stop=True),
                    waits=[tqv[s]] + bk.wr(bank))
                if j == KC - 1:
                    qvs_rdr[s] = mmbd
                tev = p.op("scalar", lambda E, j=j, W=W, bank=bank: E.copy(
                    bdraw[0][:, W - 512 * (j + 1):W - 512 * j], ps[bank][:, 0:512]),
                    waits=[mmbd] + SD["diagp"])
                bk.rd(bank, tev)
                evacs.append(tev)
            SD["diagp"] = [p.dma(
                "sync", band[s][:, 0:W].bitcast(F32),
                AP(bdraw[0].ap().tensor, W - 1 - a, [[BDP - 1, 128], [1, W]]),
                waits=evacs + [band_rdr[s]], ch="bdd0")]
            t_diag[s] = SD["diagp"]
        for s in range(4):
            a = 512 * b2 + 128 * s
            tas_hist = {}
            for kc in range(KC):
                bank = 6 if kc % 2 == 0 else 7
                j = kc % 4
                sl = slice(kc * 512, (kc + 1) * 512)
                mmac = p.op("tensor", lambda E, a=a, sl=sl, bank=bank: E.matmul(
                    ps[bank][:, 0:512], qu[:, a:a + 128], kfm[:, sl],
                    start=True, stop=True),
                    waits=bk.wr(bank))
                tac = p.op("scalar", lambda E, j=j, bank=bank: E.copy(
                    acst(j), ps[bank][:, 0:512]),
                    waits=[mmac, acst_rdr[j]])
                bk.rd(bank, tac)
                tas = p.op("vector", lambda E, s=s, sl=sl, j=j: E.tensor_tensor(
                    band[s][:, sl], acst(j), band[s][:, sl], ALU.add),
                    waits=[tac] + t_diag[s])
                acst_rdr[j] = tas
                if kc == KC - 1:
                    off = a % 512
                    tas = p.op("vector", lambda E, s=s, sl=sl, off=off: E.tensor_tensor(
                        band[s][:, sl], band[s][:, sl],
                        hmaskt[:, 512 - off:1024 - off], ALU.add),
                        waits=[tas, t_cst])
                tas_hist[kc] = tas
            tmx = p.op("vector", lambda E, s=s, W=W: E.reduce_max(
                mx[s][:], band[s][:, 0:W], AX.X),
                waits=[tas_hist[kc] for kc in range(KC)] + [mx_rdr[s]])
            tneg = p.op("vector", lambda E, s=s: E.tensor_scalar(nmx[s][:], mx[s][:], -1.0, None, ALU.mult),
                        waits=[tmx, exp_last[s]])
            mx_rdr[s] = tneg
            te = p.op("scalar", lambda E, s=s, W=W: E.activation(
                band[s][:, 0:W], band[s][:, 0:W], ACTF.Exp, bias=nmx[s][:],
                scale=1.0, accum_out=rsum[s][:]),
                waits=[tneg, rs_prev[s]] + [tas_hist[kc] for kc in range(KC)])
            for kc in range(KC):
                t_tp_ev[("exp", s, kc)] = te
            exp_last[s] = te
            t_tp_ev[("rs", s)] = p.op("vector", lambda E, s=s: E.reciprocal(risum[s][:], rsum[s][:]),
                                      waits=[te, ev_last[s]])

    def seed_acst_rdr():
        tok = t_gate_hist[-1] if t_gate_hist else None
        for j in range(4):
            acst_rdr[j] = acst_rdr[j] or tok

    def emit_tp(b2):
        for s in range(4):
            NKT = 4 * b2 + s + 1
            for g in range((NKT + 3) // 4):
                k0 = 4 * g
                gw = min(4, NKT - k0)
                bank = 6 + (g % 2)
                ttp = None
                for u in range(gw):
                    kt = k0 + u
                    ttp = p.op("tensor", lambda E, s=s, kt=kt, u=u, bank=bank: E.transpose(
                        ps[bank][:, u * 128:(u + 1) * 128].bitcast(F32R),
                        band[s][:, kt * 128:(kt + 1) * 128], identr[:]),
                        waits=[t_tp_ev[("exp", s, g)], t_idr] + (bk.wr(bank) if u == 0 else []))
                tev2 = p.op("scalar", lambda E, s=s, k0=k0, gw=gw, bank=bank: E.copy(
                    band[s][:, k0 * 128:(k0 + gw) * 128],
                    ps[bank][:, 0:gw * 128].bitcast(F32R)),
                    waits=[ttp])
                bk.rd(bank, tev2)
                for u in range(gw):
                    t_tp_ev[(s, k0 + u)] = tev2

    for b in range(NB):
        if b == 0:
            emit_scores(0)
            emit_tp(0)
        av_stop = {}
        t_avev = {}
        for q in range(4):
            for kt in range(4 * b + 4):
                i = kt % 3
                t_vb[i] = p.dma("sync", vbuf[i][:],
                                v_hbm.ap()[kt][:, q * 512:(q + 1) * 512],
                                waits=[t_vb[i], vb_rdr[i], t_vst[0], t_vst[1]], ch=f"vb{i}")
                last = None
                for s in range(4):
                    if kt > 4 * b + s:
                        continue
                    last = p.op("tensor", lambda E, s=s, kt=kt, i=i, b=b: E.matmul(
                        ps[s][:, 0:512],
                        band[s][:, kt * 128:(kt + 1) * 128],
                        vbuf[i][:],
                        start=(kt == 0), stop=(kt == 4 * b + s)),
                        waits=[t_vb[i], t_tp_ev[(s, kt)]] + (bk.wr(s) if kt == 0 else []))
                    if kt == 4 * b + s:
                        av_stop[(s, q)] = last
                vb_rdr[i] = last
            for s in range(4):
                ev = p.op("scalar", lambda E, s=s, q=q: E.activation(
                    wvt[s][:, q * 512:(q + 1) * 512], ps[s][:, 0:512],
                    ACTF.Identity, bias=0.0, scale=risum[s][:]),
                    waits=[av_stop[(s, q)], t_tp_ev[("rs", s)],
                           sq_rdr if (s == 0 and q < 2) else None])
                bk.rd(s, ev)
                t_avev[(s, q)] = ev
                if q == 3:
                    ev_last[s] = ev
        for s in range(4):
            band_rdr[s] = av_stop[(s, 3)]
            rs_prev[s] = t_tp_ev[("rs", s)]
        tg_tok = {}
        for s in range(4):
            for gh in range(2):
                ig = (s * 2 + gh) % 2
                t_gb[ig] = p.dma("sync", gbuf[ig][:],
                                 g_hbm.ap()[b * 4 + s][:, gh * 1024:(gh + 1) * 1024],
                                 waits=[t_gb[ig], t_gate_hist[-2] if len(t_gate_hist) >= 2 else None,
                                        acst_rdr[2 * ig], acst_rdr[2 * ig + 1],
                                        t_gst[0], t_gst[1]], ch=f"gb{ig}")
                tsi = p.op("scalar", lambda E, ig=ig: E.activation(gbuf[ig][:], gbuf[ig][:], ACTF.Silu),
                           waits=[t_gb[ig]])
                tg = p.op("vector", lambda E, s=s, gh=gh, ig=ig: E.tensor_mul(
                    wvt[s][:, gh * 1024:(gh + 1) * 1024],
                    wvt[s][:, gh * 1024:(gh + 1) * 1024], gbuf[ig][:]),
                    waits=[tsi, t_avev[(s, 2 * gh)], t_avev[(s, 2 * gh + 1)]])
                tg_tok[(s, gh)] = tg
                t_gate_hist.append(tg)
        t_wv_tp = {}
        for s in range(4):
            for g in range(4):
                bank = 4 + (g % 2)
                ttp = None
                for u in range(4):
                    vfc = 4 * g + u
                    ttp = p.op("tensor", lambda E, s=s, vfc=vfc, u=u, bank=bank: E.transpose(
                        ps[bank][:, u * 128:(u + 1) * 128].bitcast(F32R),
                        wvt[s][:, vfc * 128:(vfc + 1) * 128], identr[:]),
                        waits=[tg_tok[(s, g // 2)], t_idr] + (bk.wr(bank) if u == 0 else []))
                tev3 = p.op("scalar", lambda E, s=s, g=g, bank=bank: E.copy(
                    wvt[s][:, g * 512:(g + 1) * 512], ps[bank][:, 0:512].bitcast(F32R)),
                    waits=[ttp])
                bk.rd(bank, tev3)
                for u in range(4):
                    t_wv_tp[(s, 4 * g + u)] = tev3
        if b + 1 < NB:
            emit_scores(b + 1)
        t_xr = [None] * 4
        for s in range(4):
            tt = b * 4 + s
            t_xr[s] = p.dma("sync", ln1t[s][:], x_in[tt * 128:(tt + 1) * 128, :],
                            waits=ln1t_rdr[s], ch=f"xr{s}")
        o_stop = {}
        ln_add = {}
        for c in range(2):
            for vfc in range(VFC):
                jw = (c * VFC + vfc) % 2
                t_wob[jw] = p.dma("sync", wobuf[jw][:],
                                  wo_in[vfc * 128:(vfc + 1) * 128, c * 512:(c + 1) * 512],
                                  waits=[t_wob[jw], wo_rdr[jw]], ch=f"wo{jw}")
                last = None
                for s in range(4):
                    last = p.op("tensor", lambda E, s=s, vfc=vfc, jw=jw: E.matmul(
                        ps[s][:, 0:512],
                        wvt[s][:, vfc * 128:(vfc + 1) * 128],
                        wobuf[jw][:],
                        start=(vfc == 0), stop=(vfc == VFC - 1)),
                        waits=[t_wob[jw], t_wv_tp[(s, vfc)]] + (bk.wr(s) if vfc == 0 else []))
                    if vfc == VFC - 1:
                        o_stop[(s, c)] = last
                wo_rdr[jw] = last
            for s in range(4):
                ta_ = p.op("vector", lambda E, s=s, c=c: E.tensor_add(
                    ln1t[s][:, c * 512:(c + 1) * 512], ps[s][:, 0:512],
                    ln1t[s][:, c * 512:(c + 1) * 512]),
                    waits=[o_stop[(s, c)], t_xr[s]])
                bk.rd(s, ta_)
                ln_add[(s, c)] = ta_
        if b + 1 < NB:
            emit_tp(b + 1)
        for s in range(4):
            tt = b * 4 + s
            tsm = p.op("vector", lambda E, s=s: E.reduce_sum(smean[:], ln1t[s][:], AX.X),
                       waits=[ln_add[(s, 0)], ln_add[(s, 1)], t_srstd_rdr])
            tsq = p.op("scalar", lambda E, s=s: E.activation(
                wvt[0][:, 0:1024], ln1t[s][:], ACTF.Square, accum_out=smsq[:]),
                waits=[ln_add[(s, 0)], ln_add[(s, 1)], t_sqt_rdr])
            sq_rdr = tsq
            tmn = p.op("vector", lambda E: E.tensor_scalar(smean[:], smean[:], 1.0 / 1024.0, None, ALU.mult),
                       waits=[tsq, tsm])
            tms = p.op("vector", lambda E: E.tensor_scalar(smsq[:], smsq[:], 1.0 / 1024.0, None, ALU.mult),
                       waits=[tmn])
            tm2 = p.op("vector", lambda E: E.tensor_mul(svar[:], smean[:], smean[:]), waits=[tms])
            tvr = p.op("vector", lambda E: E.tensor_sub(svar[:], smsq[:], svar[:]), waits=[tm2])
            t_sqt_rdr = tvr
            tsd = p.op("scalar", lambda E: E.activation(srstd[:], svar[:], ACTF.Sqrt, bias=eps5[:]),
                       waits=[tvr, t_srstd_rdr, t_c5])
            trs2 = p.op("vector", lambda E: E.reciprocal(srstd[:], srstd[:]), waits=[tsd])
            tnorm = p.op("vector", lambda E, s=s: E.tensor_scalar(
                ln1t[s][:], ln1t[s][:], smean[:], srstd[:], ALU.subtract, ALU.mult),
                waits=[trs2])
            t_srstd_rdr = tnorm
            trm = p.op("vector", lambda E, s=s: E.reduce_max(
                tredA[:], ln1t[s][:], AX.X, apply_absolute_value=True), waits=[tnorm, t_lnmax])
            t_lnmax = p.op("vector", lambda E: E.tensor_max(lnmax[:], lnmax[:], tredA[:]),
                           waits=[trm])
            t_ln1 = p.dma("gpsimd", ln1_hbm.ap()[tt], ln1t[s][:], waits=[tnorm], ch=f"lnw{s}")
            ln1t_rdr[s] = [t_ln1, trm]
    t_ln1 = [t_ln1, t_ln1]

    def allreduce_max(tag, col_tok, col_tile, out_s, out_is, waits=()):
        cin, cout, csc = cc_bufs[tag]
        tw = p.dma("gpsimd", cin.ap().rearrange("(p o) -> p o", o=1), col_tile[:],
                   waits=[col_tok] + list(waits), ch="cc" + tag)
        tcc = p.op("gpsimd", lambda E: E.collective_compute(
            "AllReduce", ALU.max, replica_groups=[list(range(n_cores))],
            ins=[cin.ap()], outs=[cout.ap()]), waits=[tw])
        trb = p.dma("gpsimd", sc_row[:], cout.ap().rearrange("(o t) -> o t", o=1), waits=[tcc], ch="cc" + tag)
        u1 = p.op("vector", lambda E: E.reduce_max(sc_one[:], sc_row[:], AX.X), waits=[trb])
        u2 = p.op("vector", lambda E: E.tensor_mul(sc_tmp[:], sc_one[:], invqmax[:]), waits=[u1, t_invq])
        u3 = p.op("vector", lambda E: E.tensor_scalar(sc_tmp[:], sc_tmp[:], 1e-12, None, ALU.max),
                  waits=[u2])
        u4 = p.op("vector", lambda E: E.reciprocal(sc_one[:], sc_tmp[:]), waits=[u3])
        tws = p.dma("gpsimd", csc.ap()[0:1].rearrange("(p o) -> p o", o=1), sc_tmp[:],
                    waits=[u3, u4], ch="cc" + tag)
        twi = p.dma("gpsimd", csc.ap()[1:2].rearrange("(p o) -> p o", o=1), sc_one[:],
                    waits=[u4], ch="cc" + tag)
        b1e = p.dma("gpsimd", out_s[:], AP(csc, 0, [[0, 128], [1, 1]]),
                    waits=[twi], ch="cc" + tag)
        b2e = p.dma("gpsimd", out_is[:], AP(csc, 1, [[0, 128], [1, 1]]),
                    waits=[b1e], ch="cc" + tag)
        return [b1e, b2e]

    t_s1 = allreduce_max("a", t_lnmax, lnmax, scl["s1"], scl["is1"], waits=[t_ln1[0], t_ln1[1]])
    if "ln1_0" in dbg_outs:
        p.dma("sync", dbg_outs["ln1_0"], ln1_hbm.ap()[0], waits=[t_ln1[0], t_ln1[1]], ch="dbg")
        p.dma("sync", dbg_outs["ln1_31"], ln1_hbm.ap()[31], waits=[t_ln1[0], t_ln1[1]], ch="dbg")
        p.dma("sync", dbg_outs["s1"], scl["s1"][:], waits=t_s1, ch="dbg")
    if phases <= 2:
        return _finish()
    p.barrier()
    s2_stack.close()
    attn_stack.close()

    s4_stack = ExitStack()

    def sb4(name, shape, dt=F32):
        return s4_stack.enter_context(nc.sbuf_tensor(name, shape, dt))

    wslab = [sb4(f"wsl{i}", [128, D]) for i in range(2)]
    wqslab = [sb4(f"wql{i}", [128, D], F32R) for i in range(2)]
    wa = sb4("wa", [128, D])
    smax = sb4("smax", [128, 32])
    s_t = sb4("s_t", [128, 32])
    sinv = sb4("sinv", [128, 32])
    wstg = [sb4(f"wstg{i}", [128, 512], F32R) for i in range(2)]

    S = {"wa_rdr": None}
    slab_rdr = [None, None]
    wq_rdr = [None, None]
    stg_rdr = [None, None]

    def lpbq_load(idx, src_slice, in_cols, extra=()):
        i = idx % 2
        return p.dma("sync", wslab[i][:, 0:in_cols], src_slice,
                     waits=[slab_rdr[i]] + list(extra), ch=f"wsl{i}")

    def lpbq_dequant(idx, t_w, in_cols):
        i = idx % 2
        nb = in_cols // 32
        w3 = wslab[i][:, 0:in_cols].rearrange("p (b k) -> p b k", k=32)
        trd = p.op("vector", lambda E, w3=w3, nb=nb: E.reduce_max(
            smax[:, 0:nb], w3, AX.X, apply_absolute_value=True), waits=[t_w, S["wa_rdr"]])
        ts1 = p.op("vector", lambda E, nb=nb: E.tensor_scalar(
            s_t[:, 0:nb], smax[:, 0:nb], INV7_HI, None, ALU.mult), waits=[trd])
        ts2 = p.op("vector", lambda E, nb=nb: E.scalar_tensor_tensor(
            s_t[:, 0:nb], smax[:, 0:nb], INV7_LO, s_t[:, 0:nb], ALU.mult, ALU.add),
            waits=[ts1])
        ts3 = p.op("vector", lambda E, nb=nb: E.tensor_scalar(
            s_t[:, 0:nb], s_t[:, 0:nb], 1e-12, None, ALU.max), waits=[ts2])
        tsi = p.op("vector", lambda E, nb=nb: E.reciprocal(sinv[:, 0:nb], s_t[:, 0:nb]),
                   waits=[ts3])
        sinv_b = AP(sinv.ap().tensor, 0, [[32, 128], [1, nb], [0, 32]])
        st_b = AP(s_t.ap().tensor, 0, [[32, 128], [1, nb], [0, 32]])
        wa3 = wa[:, 0:in_cols].rearrange("p (b k) -> p b k", k=32)
        tm1 = p.op("vector", lambda E, w3=w3, sinv_b=sinv_b, wa3=wa3: E.tensor_tensor(
            wa3, w3, sinv_b, ALU.mult), waits=[tsi, S["wa_rdr"]])
        tm2 = p.op("vector", lambda E, in_cols=in_cols: E.tensor_scalar(
            wa[:, 0:in_cols], wa[:, 0:in_cols], biasA[:], None, ALU.add),
            waits=[tm1, t_biasA])
        tm3 = p.op("vector", lambda E, in_cols=in_cols: E.tensor_scalar(
            wa[:, 0:in_cols], wa[:, 0:in_cols], biasA[:], None, ALU.subtract),
            waits=[tm2])
        wq3 = wqslab[i][:, 0:in_cols].rearrange("p (b k) -> p b k", k=32)
        tm4 = p.op("vector", lambda E, wq3=wq3, wa3=wa3, st_b=st_b: E.tensor_tensor(
            wq3, wa3, st_b, ALU.mult), waits=[tm3, wq_rdr[i]])
        S["wa_rdr"] = tm4
        slab_rdr[i] = tm1
        return tm4

    lt_rdr = [None, None]
    lt_rdy = {}

    def emit_T(c, t_deq):
        i = c % 2
        bks = (4, 5) if i == 0 else (6, 7)
        last_tp = {}
        for dc in range(DC):
            bank = bks[dc // 4]
            ttp = p.op("tensor", lambda E, i=i, dc=dc, bank=bank: E.transpose(
                ps[bank][:, (dc % 4) * 128:((dc % 4) + 1) * 128].bitcast(F32R),
                wqslab[i][:, dc * 128:(dc + 1) * 128], identr[:]),
                waits=[t_deq, t_idr] + (bk.wr(bank) if dc % 4 == 0 else []))
            last_tp[bank] = ttp
            if dc == DC - 1:
                wq_rdr[i] = ttp
        evs = []
        for h, bank in enumerate(bks):
            tev = p.op("scalar", lambda E, i=i, h=h, bank=bank: E.copy(
                lt[i][:, h * 512:(h + 1) * 512], ps[bank][:, 0:512].bitcast(F32R)),
                waits=[last_tp[bank], lt_rdr[i]])
            bk.rd(bank, tev)
            evs.append(tev)
        lt_rdy[c] = evs

    chunks = ([("up", wup_in, upraw_hbm, ifc) for ifc in range(IFC)] +
              [("gate", wgate_in, gateraw_hbm, ifc) for ifc in range(IFC)])
    NCH = len(chunks)
    t_load = [None] * NCH

    def chunk_src(c):
        _, srcw, _, ifc = chunks[c]
        return srcw[ifc * 128:(ifc + 1) * 128, :]

    x1_stack = ExitStack()

    def sbx(name, shape, dt=F32):
        return x1_stack.enter_context(nc.sbuf_tensor(name, shape, dt))

    lt = [sbx(f"lt{i}", [128, D], F32R) for i in range(2)]
    rawst = [sbx(f"raw{i}", [128, T]) for i in range(2)]
    lnt = [sbx(f"lnt{i}", [128, D]) for i in range(2)]
    ast = [sbx(f"ast{i}", [128, 128]) for i in range(2)]
    x1 = [sbx(f"x1_{dc}", [128, T], F32R) for dc in range(DC)]

    t_load[0] = lpbq_load(0, chunk_src(0), D)
    t_load[1] = lpbq_load(1, chunk_src(1), D)
    emit_T(0, lpbq_dequant(0, t_load[0], D))
    lnt_rdr = [None, None]
    ast_rdr = [None, None]
    t_x1 = {}
    for tt in range(NT):
        i = tt % 2
        t_ln = p.dma("sync", lnt[i][:], ln1_hbm.ap()[tt], waits=[lnt_rdr[i], t_ln1[0], t_ln1[1]], ch=f"lnr{i}")
        last_tp = None
        for dc in range(DC):
            bank = 6 + (dc % 2)
            ttp = p.op("tensor", lambda E, i=i, dc=dc, bank=bank: E.transpose(
                ps[bank][:, 0:128], lnt[i][:, dc * 128:(dc + 1) * 128], identt[:]),
                waits=[t_ln] + bk.wr(bank))
            last_tp = ttp
            j = dc % 2
            ta = p.op("scalar", lambda E, j=j, bank=bank: E.activation(
                ast[j][:], ps[bank][:, 0:128], ACTF.Identity,
                bias=biasA[:], scale=scl["is1"][:]),
                waits=[ttp, t_biasA, ast_rdr[j]] + t_s1)
            bk.rd(bank, ta)
            tx = p.op("vector", lambda E, tt=tt, dc=dc, j=j: E.tensor_scalar(
                x1[dc][:, tt * 128:(tt + 1) * 128], ast[j][:], biasA[:], None, ALU.subtract),
                waits=[ta])
            ast_rdr[j] = tx
            t_x1[(tt, dc)] = tx
        lnt_rdr[i] = last_tp

    t_raw = [None, None]
    t_upmax = t_mx0["up"]
    t_gatemax = t_mx0["gate"]
    t_redprev = None
    for c in range(NCH):
        i = c % 2
        name, _, raw_h, ifc = chunks[c]
        if c + 2 < NCH:
            t_load[c + 2] = lpbq_load(c + 2, chunk_src(c + 2), D)
        if c + 1 < NCH:
            t_deq1 = lpbq_dequant(c + 1, t_load[c + 1], D)
        mm_stop = {}
        lastmm = None
        evs = []
        for half in range(2):
            for dc in range(DC):
                for q in range(4):
                    tc = half * 4 + q
                    tok = p.op("tensor", lambda E, i=i, dc=dc, q=q, tc=tc: E.matmul(
                        ps[q][:, 0:512], lt[i][:, dc * 128:(dc + 1) * 128],
                        x1[dc][:, tc * 512:(tc + 1) * 512],
                        start=(dc == 0), stop=(dc == DC - 1)),
                        waits=lt_rdy[c] +
                              [t_x1[(tc * 4 + k, dc)] for k in range(4)] +
                              (bk.wr(q) if dc == 0 else []))
                    if dc == DC - 1:
                        mm_stop[tc] = tok
                    lastmm = tok
            for q in range(4):
                tc = half * 4 + q
                ev = p.op("scalar", lambda E, i=i, tc=tc, q=q: E.activation(
                    rawst[i][:, tc * 512:(tc + 1) * 512], ps[q][:, 0:512],
                    ACTF.Identity, bias=0.0, scale=scl["s1"][:]),
                    waits=[mm_stop[tc], t_raw[i]] + t_s1)
                bk.rd(q, ev)
                evs.append(ev)
            if half == 0 and c + 1 < NCH:
                emit_T(c + 1, t_deq1)
        lt_rdr[i] = lastmm
        trd = p.op("vector", lambda E, i=i: E.reduce_max(
            tredA[:], rawst[i][:], AX.X, apply_absolute_value=True),
            waits=evs + [t_redprev])
        if name == "up":
            t_upmax = p.op("vector", lambda E: E.tensor_max(upmax[:], upmax[:], tredA[:]),
                           waits=[trd, t_upmax])
            t_redprev = t_upmax
        else:
            t_gatemax = p.op("vector", lambda E: E.tensor_max(gatemax[:], gatemax[:], tredA[:]),
                             waits=[trd, t_gatemax])
            t_redprev = t_gatemax
        t_raw[i] = p.dma("scalar", raw_h.ap()[ifc], rawst[i][:], waits=evs, ch=f"rww{i}")
    x1_stack.close()

    idx = NCH
    for oc in range(DC):
        for sc in range(3):
            t_w = lpbq_load(idx, wdown_in[oc * 128:(oc + 1) * 128, sc * 1024:(sc + 1) * 1024], D)
            tm4 = lpbq_dequant(idx, t_w, D)
            i = idx % 2
            bks = (4, 5) if i == 0 else (6, 7)
            last_tp = {}
            for dc in range(DC):
                bank = bks[dc // 4]
                ttp = p.op("tensor", lambda E, i=i, dc=dc, bank=bank: E.transpose(
                    ps[bank][:, (dc % 4) * 128:((dc % 4) + 1) * 128].bitcast(F32R),
                    wqslab[i][:, dc * 128:(dc + 1) * 128], identr[:]),
                    waits=[tm4, t_idr] + (bk.wr(bank) if dc % 4 == 0 else []))
                last_tp[bank] = ttp
                if dc == DC - 1:
                    wq_rdr[i] = ttp
            for h, bank in enumerate(bks):
                j = (idx * 2 + h) % 2
                tev = p.op("scalar", lambda E, j=j, bank=bank: E.copy(
                    wstg[j][:], ps[bank][:, 0:512].bitcast(F32R)),
                    waits=[last_tp[bank], stg_rdr[j]])
                bk.rd(bank, tev)
                dst = AP(wdownT_hbm, (sc * 8 + 4 * h) * 128 * D + oc * 128,
                         [[D, 128], [128 * D, 4], [1, 128]])
                stg_rdr[j] = p.dma("scalar", dst,
                                   wstg[j][:].rearrange("p (b k) -> p b k", k=128),
                                   waits=[tev], ch=f"wstg{j}")
            idx += 1
    t_wrow = [stg_rdr[0], stg_rdr[1]]

    t_s2 = allreduce_max("b1", t_upmax, upmax, scl["s2"], scl["is2"], waits=[t_raw[0], t_raw[1]])
    t_s3 = allreduce_max("b2", t_gatemax, gatemax, scl["s3"], scl["is3"], waits=t_s2)
    if "up0" in dbg_outs:
        p.dma("sync", dbg_outs["up0"], upraw_hbm.ap()[0], waits=[t_raw[0], t_raw[1]], ch="dbg")
        p.dma("sync", dbg_outs["gate0"], gateraw_hbm.ap()[0], waits=[t_raw[0], t_raw[1]], ch="dbg")
        p.dma("sync", dbg_outs["s2"], scl["s2"][:], waits=t_s2, ch="dbg")
        p.dma("sync", dbg_outs["s3"], scl["s3"][:], waits=t_s3, ch="dbg")
    if phases <= 4:
        return _finish()

    s4b_stack = ExitStack()

    def sb4b(name, shape, dt=F32):
        return s4b_stack.enter_context(nc.sbuf_tensor(name, shape, dt))

    gst = [sb4b(f"p2g{i}", [128, T]) for i in range(2)]
    ust = [sb4b(f"p3u{i}", [128, T]) for i in range(2)]
    aq = [sb4b(f"aq{i}", [128, T]) for i in range(2)]
    gq = [sb4b(f"gq{i}", [128, T]) for i in range(2)]
    sg = [sb4b(f"sg{i}", [128, T]) for i in range(2)]
    chain_hist = {}

    def gate_chain(i, waits):
        pw = [chain_hist.get(("v", i)), chain_hist.get(("s", i)), chain_hist.get(("tsp", i))]
        ta = p.op("scalar", lambda E, i=i: E.activation(
            aq[i][:], gst[i][:], ACTF.Identity, bias=biasA[:], scale=scl["is3"][:]),
            waits=list(waits) + t_s3 + [t_biasA] + pw)
        chain_hist[("gstrd", i)] = ta
        tg = p.op("vector", lambda E, i=i: E.tensor_scalar(
            gq[i][:], aq[i][:], biasA[:], scl["s3"][:], ALU.subtract, ALU.mult),
            waits=[ta] + pw)
        tsg = p.op("scalar", lambda E, i=i: E.activation(sg[i][:], gq[i][:], ACTF.Sigmoid), waits=[tg])
        tq1 = p.op("scalar", lambda E, i=i: E.activation(aq[i][:], sg[i][:], ACTF.Identity,
                                                         bias=biasA[:], scale=c64k[:]),
                   waits=[tsg, tg, t_c64k])
        tq2 = p.op("vector", lambda E, i=i: E.tensor_scalar(
            sg[i][:], aq[i][:], biasA[:], 65535.0, ALU.subtract, ALU.min), waits=[tq1])
        tt_ = p.op("vector", lambda E, i=i: E.scalar_tensor_tensor(
            sg[i][:], sg[i][:], 2.0 ** -16, gq[i][:], ALU.mult, ALU.mult), waits=[tq2])
        chain_hist[("s", i)] = tq1
        return tt_

    t_p2 = [None, None]
    t_sp = [None, None]
    t_tmax = t_mx0["t"]
    for ifc in range(IFC):
        i = ifc % 2
        t_p2[i] = p.dma("sync", gst[i][:], gateraw_hbm.ap()[ifc],
                        waits=[t_p2[i], chain_hist.get(("gstrd", i)), t_raw[0], t_raw[1]],
                        ch=f"g2r{i}")
        tt_ = gate_chain(i, [t_p2[i]])
        trd = p.op("vector", lambda E, i=i: E.reduce_max(
            tredA[:], sg[i][:], AX.X, apply_absolute_value=True), waits=[tt_, t_tmax])
        t_tmax = p.op("vector", lambda E: E.tensor_max(tmaxc[:], tmaxc[:], tredA[:]),
                      waits=[trd])
        chain_hist[("v", i)] = trd
        t_sp[i] = p.dma("scalar", tq_hbm.ap()[ifc], sg[i][:], waits=[tt_, t_sp[i]], ch=f"tw{i}")
        chain_hist[("tsp", i)] = t_sp[i]
    t_s4 = allreduce_max("c", t_tmax, tmaxc, scl["s4"], scl["is4"])

    t_p3g = [None, None]
    t_p3u = [None, None]
    t_p3o = [None, None]
    t_omax = t_mx0["o"]
    chain_hist.clear()
    for ifc in range(IFC):
        i = ifc % 2
        t_p3g[i] = p.dma("sync", gst[i][:], tq_hbm.ap()[ifc],
                         waits=[t_p3g[i], chain_hist.get(("v", i)), t_p3o[i], t_tmax,
                                t_sp[0], t_sp[1]],
                         ch=f"g2r{i}")
        t_p3u[i] = p.dma("sync", ust[i][:], upraw_hbm.ap()[ifc],
                         waits=[t_p3u[i], chain_hist.get(("ust", i)), t_tmax], ch=f"u3r{i}")
        ta2 = p.op("scalar", lambda E, i=i: E.activation(aq[i][:], gst[i][:], ACTF.Identity,
                                                         bias=biasA[:], scale=scl["is4"][:]),
                   waits=[t_p3g[i], chain_hist.get(("aqrd", i)), t_biasA] + t_s4)
        tg2 = p.op("vector", lambda E, i=i: E.tensor_scalar(
            sg[i][:], aq[i][:], biasA[:], scl["s4"][:], ALU.subtract, ALU.mult),
            waits=[ta2, chain_hist.get(("sgrd", i))])
        ta3 = p.op("scalar", lambda E, i=i: E.activation(aq[i][:], ust[i][:], ACTF.Identity,
                                                         bias=biasA[:], scale=scl["is2"][:]),
                   waits=[tg2, t_p3u[i]] + t_s2)
        chain_hist[("ust", i)] = ta3
        tu = p.op("vector", lambda E, i=i: E.tensor_scalar(
            gq[i][:], aq[i][:], biasA[:], scl["s2"][:], ALU.subtract, ALU.mult),
            waits=[ta3, chain_hist.get(("gqrd", i))])
        chain_hist[("aqrd", i)] = tu
        top = p.op("vector", lambda E, i=i: E.tensor_mul(gst[i][:], sg[i][:], gq[i][:]),
                   waits=[tu, tg2, t_p3o[i]])
        chain_hist[("sgrd", i)] = top
        chain_hist[("gqrd", i)] = top
        trd = p.op("vector", lambda E, i=i: E.reduce_max(
            tredA[:], gst[i][:], AX.X, apply_absolute_value=True), waits=[top, t_omax])
        t_omax = p.op("vector", lambda E: E.tensor_max(omaxc[:], omaxc[:], tredA[:]),
                      waits=[trd])
        chain_hist[("v", i)] = trd
        t_p3o[i] = p.dma("scalar", opre_hbm.ap()[ifc], gst[i][:], waits=[top], ch=f"opw{i}")
    t_s5 = allreduce_max("d", t_omax, omaxc, scl["s5"], scl["is5"], waits=[t_p3o[0], t_p3o[1]])
    if "op0" in dbg_outs:
        p.dma("sync", dbg_outs["op0"], opre_hbm.ap()[0], waits=[t_p3o[0], t_p3o[1]], ch="dbg")
        p.dma("sync", dbg_outs["s4"], scl["s4"][:], waits=t_s4, ch="dbg")
        p.dma("sync", dbg_outs["s5"], scl["s5"][:], waits=t_s5, ch="dbg")
    if phases <= 5:
        return _finish()
    p.barrier()
    s4b_stack.close()
    s4_stack.close()

    s5_stack = ExitStack()

    def sb5(name, shape, dt=F32):
        return s5_stack.enter_context(nc.sbuf_tensor(name, shape, dt))

    wdt = [sb5(f"wdt{ifc}", [128, D], F32R) for ifc in range(IFC)]
    t_wdt = [None] * IFC
    for ifc in range(IFC):
        t_wdt[ifc] = p.dma("sync", wdt[ifc][:], wdownT_hbm.ap()[ifc],
                           waits=[t_wrow[0], t_wrow[1]], ch=f"wdtg{ifc % 4}")
    for ifc in range(IFC):
        t_wdt[ifc] = t_wdt[IFC - 4 + (ifc % 4)]
    ob = [sb5(f"ob{i}", [128, 256]) for i in range(4)]
    rb = [sb5(f"rb{i}", [128, 256], F32R) for i in range(4)]
    lnt2 = [sb5(f"lnt2{i}", [128, D]) for i in range(2)]
    yt = [sb5(f"yt{i}", [128, D]) for i in range(2)]
    sqt2 = sb5("sqt2", [128, D])
    m_a = sb5("m_a", [128, 1])
    m_b = sb5("m_b", [128, 1])
    m_c = sb5("m_c", [128, 1])
    m_d = sb5("m_d", [128, 1])
    t_ob = [None] * 4
    t_rb = [None] * 4
    rb_rdr = [None] * 4
    t_yt = [None, None]
    t_ln2 = [None, None]
    t_sq2_rdr = None
    t_md_rdr = None
    ln2_rdr = [None, None]
    for tt in range(0, NT, 2):
        u = tt // 2
        iy = u % 2
        mm_stop = {}
        for k in range(2):
            t_ln2[k] = p.dma("gpsimd", lnt2[k][:], ln1_hbm.ap()[tt + k],
                             waits=[t_ln2[k], ln2_rdr[k]], ch=f"l2r{k}")
        for ifc in range(IFC):
            i = ifc % 4
            t_ob[i] = p.dma("sync", ob[i][:],
                            AP(opre_hbm, ifc * 128 * T + tt * 128, [[T, 128], [1, 256]]),
                            waits=[t_rb[i], rb_rdr[i], t_p3o[0], t_p3o[1]], ch=f"ob{i}")
            ta = p.op("scalar", lambda E, i=i: E.activation(
                ob[i][:], ob[i][:], ACTF.Identity, bias=biasA[:], scale=scl["is5"][:]),
                waits=[t_ob[i]] + t_s5)
            t_rb[i] = p.op("vector", lambda E, i=i: E.tensor_scalar(
                rb[i][:], ob[i][:], biasA[:], None, ALU.subtract),
                waits=[ta, rb_rdr[i]])
            last = None
            for k in range(2):
                for c in range(2):
                    bank = 4 * iy + 2 * k + c
                    last = p.op("tensor", lambda E, i=i, k=k, c=c, ifc=ifc, bank=bank: E.matmul(
                        ps[bank][:, 0:512], rb[i][:, k * 128:(k + 1) * 128],
                        wdt[ifc][:, c * 512:(c + 1) * 512],
                        start=(ifc == 0), stop=(ifc == IFC - 1)),
                        waits=[t_rb[i], t_wdt[ifc]] + (bk.wr(bank) if ifc == 0 else []))
                    if ifc == IFC - 1:
                        mm_stop[(k, c)] = last
            rb_rdr[i] = last
        for k in range(2):
            ty = tt + k
            tadds = []
            for c in range(2):
                bank = 4 * iy + 2 * k + c
                ta_ = p.op("vector", lambda E, k=k, c=c, bank=bank: E.scalar_tensor_tensor(
                    yt[k][:, c * 512:(c + 1) * 512], ps[bank][:, 0:512], scl["s5"][:],
                    lnt2[k][:, c * 512:(c + 1) * 512], ALU.mult, ALU.add),
                    waits=[mm_stop[(k, c)], t_ln2[k], t_yt[k]] + t_s5)
                bk.rd(bank, ta_)
                tadds.append(ta_)
                ln2_rdr[k] = ta_
            tsm = p.op("vector", lambda E, k=k: E.reduce_sum(m_a[:], yt[k][:], AX.X), waits=tadds)
            tsq = p.op("scalar", lambda E, k=k: E.activation(
                sqt2[:], yt[k][:], ACTF.Square, accum_out=m_b[:]),
                waits=tadds + [t_sq2_rdr])
            tmn = p.op("vector", lambda E: E.tensor_scalar(m_a[:], m_a[:], 1.0 / 1024.0, None, ALU.mult),
                       waits=[tsq, tsm])
            tms = p.op("vector", lambda E: E.tensor_scalar(m_b[:], m_b[:], 1.0 / 1024.0, None, ALU.mult),
                       waits=[tmn])
            tm2 = p.op("vector", lambda E: E.tensor_mul(m_c[:], m_a[:], m_a[:]), waits=[tms])
            tvr = p.op("vector", lambda E: E.tensor_sub(m_c[:], m_b[:], m_c[:]), waits=[tm2])
            t_sq2_rdr = tvr
            tsd = p.op("scalar", lambda E: E.activation(m_d[:], m_c[:], ACTF.Sqrt, bias=eps5[:]),
                       waits=[tvr, t_md_rdr, t_c5])
            trs = p.op("vector", lambda E: E.reciprocal(m_d[:], m_d[:]), waits=[tsd])
            tnorm = p.op("vector", lambda E, k=k: E.tensor_scalar(
                yt[k][:], yt[k][:], m_a[:], m_d[:], ALU.subtract, ALU.mult), waits=[trs])
            t_md_rdr = tnorm
            t_yt[k] = p.dma("gpsimd", y_out[ty * 128:(ty + 1) * 128, :], yt[k][:],
                            waits=[tnorm], ch=f"yw{k}")

    final_waits = [(k, v) for k, v in p.cnt.items() if k.startswith("D:") and v > 0]
    for e in ("vector", "scalar", "tensor"):
        final_waits.append(p.fence(e))
    p.waits_only("gpsimd", final_waits)

    with nc.Block() as block:
        p.build_block(block)
    s5_stack.close()
    g_stack.close()
    return nc


_CONSTS = None


def _in_maps(inputs, n=8):
    global _CONSTS
    if _CONSTS is None:
        _CONSTS = _host_constants()
    embT, hmask, ident = _CONSTS
    x = np.ascontiguousarray(np.asarray(inputs["L0_attn_x"], dtype=np.float32))
    common = dict(
        g_ln=np.ascontiguousarray(inputs["L0_attn_g_ln"], dtype=np.float32),
        wq=np.ascontiguousarray(inputs["L0_attn_W_q"], dtype=np.float32),
        wk=np.ascontiguousarray(inputs["L0_attn_W_k"], dtype=np.float32),
        wv=np.ascontiguousarray(inputs["L0_attn_W_v"], dtype=np.float32),
        wg=np.ascontiguousarray(inputs["L0_attn_W_g"], dtype=np.float32),
        wr=np.ascontiguousarray(inputs["L0_attn_W_r"], dtype=np.float32),
        xlu=np.ascontiguousarray(inputs["L0_attn_xl_u"], dtype=np.float32),
        xlv=np.ascontiguousarray(inputs["L0_attn_xl_v"], dtype=np.float32),
        wo=np.ascontiguousarray(inputs["L0_attn_W_o"], dtype=np.float32),
        wup=np.ascontiguousarray(inputs["L0_ffn_w_up"], dtype=np.float32),
        wgate=np.ascontiguousarray(inputs["L0_ffn_w_gate"], dtype=np.float32),
        wdown=np.ascontiguousarray(inputs["L0_ffn_w_down"], dtype=np.float32),
        embT=embT, hmask=hmask, ident=ident,
    )
    return [dict(common, x=x[c]) for c in range(n)]


def kernel(L0_attn_x, L0_attn_g_ln, L0_attn_W_q, L0_attn_W_k, L0_attn_W_v,
           L0_attn_W_g, L0_attn_W_r, L0_attn_xl_u, L0_attn_xl_v, L0_attn_W_o,
           L0_ffn_w_gate, L0_ffn_w_up, L0_ffn_w_down):
    n = 8
    in_maps = _in_maps(dict(
        L0_attn_x=L0_attn_x, L0_attn_g_ln=L0_attn_g_ln, L0_attn_W_q=L0_attn_W_q,
        L0_attn_W_k=L0_attn_W_k, L0_attn_W_v=L0_attn_W_v, L0_attn_W_g=L0_attn_W_g,
        L0_attn_W_r=L0_attn_W_r, L0_attn_xl_u=L0_attn_xl_u, L0_attn_xl_v=L0_attn_xl_v,
        L0_attn_W_o=L0_attn_W_o, L0_ffn_w_gate=L0_ffn_w_gate, L0_ffn_w_up=L0_ffn_w_up,
        L0_ffn_w_down=L0_ffn_w_down), n)
    nc = build(n_cores=n)
    res = run_bass_kernel_spmd(nc, in_maps, core_ids=list(range(n)))
    return np.stack([r["y"] for r in res.results]).astype(np.float32)
